# version 19
# baseline (speedup 1.0000x reference)
import numpy as np
from contextlib import ExitStack

import concourse.bass as bass
import concourse.mybir as mybir
from concourse.bass_utils import run_bass_kernel_spmd
from concourse.alu_op_type import AluOpType

F32 = mybir.dt.float32
BF16 = mybir.dt.bfloat16
AF = mybir.ActivationFunctionType
AX = mybir.AxisListType

SEQ = 8192
D = 1024
NH = 16
CORE_T = 2048
HALF_T = 1024
TPH = 8
EXT = HALF_T + 6
NE = 8
HF = 4096
NG = 4
GCOLS = 1024
NJ = 8
NEG = -1.0e30

ENGS = ("pe", "act", "dve", "pool", "sp")
_DEC = {"pe": "tensor", "act": "scalar", "dve": "vector", "pool": "gpsimd", "sp": "sync"}
DMA_SEMS = ("sSET", "sSETp", "sXB0", "sXB1", "sWQ0", "sWQ1", "sXO0", "sXO1",
             "sW10", "sW11", "sW20", "sW21", "sO0", "sO1")


class Prog:

    def __init__(self, nc, stack):
        self.nc = nc
        names = list(DMA_SEMS) + ["sPE", "sACT", "sDVE", "sPL"]
        self.h = {n: stack.enter_context(nc.semaphore(n)) for n in names}
        self.v = {n: 0 for n in names}
        self.q = {e: [] for e in ENGS}
        self.waited = {e: {n: 0 for n in names} for e in ENGS}
        self.floor = {n: 0 for n in names}

    def op(self, eng, fn, waits=(), inc=None, k=1):
        ev = None
        if inc is not None:
            self.v[inc] += k
            ev = (inc, self.v[inc])
        ws = tuple(w for w in waits if w is not None)
        self.q[eng].append((fn, ws, inc, k))
        return ev

    def flush(self, name):
        q = self.q
        self.q = {e: [] for e in ENGS}
        with self.nc.Block(name) as blk:
            for eng in ENGS:
                ops = q[eng]
                if not ops:
                    continue

                def body(engobj, ops=ops, eng=eng):
                    for fn, waits, incname, k in ops:
                        for (sn, val) in waits:
                            if val <= self.floor[sn]:
                                continue
                            if self.waited[eng][sn] >= val:
                                continue
                            engobj.wait_ge(self.h[sn], val)
                            self.waited[eng][sn] = val
                        inst = fn(engobj)
                        if incname is not None:
                            inst.then_inc(self.h[incname], k)

                getattr(blk, _DEC[eng])(body)
        for n in self.v:
            if n not in DMA_SEMS:
                self.floor[n] = self.v[n]


def build_nc(stage="full", debug=False):
    nc = bass.Bass("TRN2", target_bir_lowering=False)
    dbg = {}
    if debug:
        dbg["qT"] = nc.dram_tensor("dbg_qT", [128, 8 * HALF_T], BF16, kind="ExternalOutput")
        dbg["kT"] = nc.dram_tensor("dbg_kT", [128, 8 * EXT], BF16, kind="ExternalOutput")
        dbg["v"] = nc.dram_tensor("dbg_v", [128, 9 * D], BF16, kind="ExternalOutput")
        dbg["o0"] = nc.dram_tensor("dbg_o0", [128, D], BF16, kind="ExternalOutput")
        dbg["xt0"] = nc.dram_tensor("dbg_xt0", [128, 8 * EXT], BF16, kind="ExternalOutput")
        dbg["scr0"] = nc.dram_tensor("dbg_scr0", [128, 16], F32, kind="ExternalOutput")
        dbg["x1f"] = nc.dram_tensor("dbg_x1f", [128, 8 * HALF_T], F32, kind="ExternalOutput")
        dbg["r20"] = nc.dram_tensor("dbg_r20", [128, D], F32, kind="ExternalOutput")
        dbg["oT0"] = nc.dram_tensor("dbg_oT0", [128, D], BF16, kind="ExternalOutput")

    xh_d = nc.dram_tensor("xh", [CORE_T + 6, D], F32, kind="ExternalInput")
    wq_d = nc.dram_tensor("wq", [D, D], F32, kind="ExternalInput")
    wk_d = nc.dram_tensor("wk", [D, D], F32, kind="ExternalInput")
    wv_d = nc.dram_tensor("wv", [D, D], F32, kind="ExternalInput")
    wo_d = nc.dram_tensor("wo", [D, D], F32, kind="ExternalInput")
    w1_d = nc.dram_tensor("w1", [NE, D, HF], F32, kind="ExternalInput")
    w2_d = nc.dram_tensor("w2", [NE, HF, D], F32, kind="ExternalInput")
    gw_d = nc.dram_tensor("gw_r", [128, NE * 8], F32, kind="ExternalInput")
    b1pc_d = nc.dram_tensor("b1_pc", [128, NE * 32], F32, kind="ExternalInput")
    bqkpc_d = nc.dram_tensor("bqk_pc", [128, 16], F32, kind="ExternalInput")
    b2_d = nc.dram_tensor("b2", [NE, D], F32, kind="ExternalInput")
    bnd_d = nc.dram_tensor("bias_nd", [65, 3 * D + 16], F32, kind="ExternalInput")
    masks_d = nc.dram_tensor("masks", [128, 3 * 134], F32, kind="ExternalInput")
    consts_d = nc.dram_tensor("consts", [128, 4], F32, kind="ExternalInput")
    ident_d = nc.dram_tensor("ident", [128, 128], F32, kind="ExternalInput")
    out_d = nc.dram_tensor("out", [CORE_T, D], F32, kind="ExternalOutput")

    with ExitStack() as S0:
        sb = lambda n, s, d: S0.enter_context(nc.sbuf_tensor("s_" + n, s, d))
        p = Prog(nc, S0)

        masks = sb("masks", [128, 3 * 134], F32)
        id_bf = sb("id_bf", [128, 128], BF16)
        id_f = sb("id_f", [128, 128], F32)
        bnd = sb("bnd", [65, 3 * D + 16], F32)
        consts = sb("consts", [128, 4], F32)
        b1pc = sb("b1pc", [128, NE * 32], F32)
        bqkpc = sb("bqkpc", [128, 16], F32)
        b2sb = sb("b2sb", [40, D], F32)
        gwb = sb("gwb", [128, NE * 8], BF16)
        xt1 = sb("xt1", [128, 8 * HALF_T], BF16)
        x1f = sb("x1f", [128, 8 * HALF_T], F32)
        comb = sb("comb", [128, TPH * NE], F32)
        St = sb("St", [128, 2 * NH], F32)
        rSt = sb("rSt", [128, 2 * NH], F32)
        scr = sb("scr", [128, 16], F32)
        smb = sb("smb", [128, 48], F32)
        r2 = sb("r2", [128, 2 * D], F32)
        ostg = sb("ostg", [128, 2 * D], F32)
        xb = sb("xb", [128, 2 * D], BF16)
        wbuf = [sb("wbuf0", [128, 8 * D], BF16), sb("wbuf1", [128, 8 * D], BF16)]

        eps_ap = consts[:, 0:1]
        neg_ap = consts[:, 1:2]

        def ones0(n):
            return bnd[0:1, D:D + n]

        def ones32(n):
            return bnd[32:33, D:D + n]

        def ones64(n):
            return bnd[64:65, D:D + n]

        ev_setup = None
        for dst, src in ((masks[:], masks_d.ap()), (bnd[:], bnd_d.ap()),
                         (consts[:], consts_d.ap()),
                         (b1pc[:], b1pc_d.ap()), (bqkpc[:], bqkpc_d.ap()),
                         (b2sb[32:40, 0:D], b2_d.ap()),
                         (id_f[:], ident_d.ap())):
            ev_setup = p.op("sp", lambda e, dst=dst, src=src: e.dma_start(dst, src),
                            inc="sSET", k=16)
        p.op("pool", lambda e: e.dma_start(id_bf[:], ident_d.ap()), inc="sSETp", k=16)
        p.op("pool", lambda e: e.dma_start(gwb[:], gw_d.ap()), inc="sSETp", k=16)
        ev_setup = ("sSET", p.v["sSET"])
        ev_setup_p = ("sSETp", p.v["sSETp"])
        ev_idbf = ev_setup_p
        ev_gwb = ev_setup_p
        p.flush("setup")

        store_ev = [None] * 16
        moe_last = {"mm1": None, "mm2": None, "stt10": None, "stt11": None,
                    "hrelu14": None, "hrelu15": None}
        pre = {}

        def emit_prefetch(h, weights=True):
            B_ = h * HALF_T
            for t in range(2):
                rows = 128
                src_ = xh_d.ap()[B_ + t * 128: B_ + t * 128 + rows, :]
                pre[(h, "xld", t)] = p.op(
                    "pool",
                    lambda e, t=t, src_=src_:
                    e.dma_start(xb[0:128, (t % 2) * D:(t % 2) * D + D], src_),
                    inc="sXB%d" % (t % 2), k=16)
            if not weights:
                return
            pre[(h, "wq")] = p.op("pool", lambda e: e.dma_start(
                wbuf[0][:].rearrange("p (c n) -> p c n", c=8),
                wq_d.ap().rearrange("(c p) n -> p c n", p=128)), inc="sWQ0", k=16)
            pre[(h, "wk")] = p.op("pool", lambda e: e.dma_start(
                wbuf[1][:].rearrange("p (c n) -> p c n", c=8),
                wk_d.ap().rearrange("(c p) n -> p c n", p=128)), inc="sWQ1", k=16)

        emit_prefetch(0, weights=False)

        for half in range(2):
            B = half * HALF_T

            with ExitStack() as S1:
                sb1 = lambda n, s, d: S1.enter_context(nc.sbuf_tensor("h%d_" % half + n, s, d))
                ps1 = lambda n, s, d: S1.enter_context(nc.psum_tensor("h%d_" % half + n, s, d))
                xt_ext = sb1("xt_ext", [128, 8 * EXT], BF16)
                qT = sb1("qT", [128, 8 * HALF_T], BF16)
                kT = sb1("kT", [128, 8 * EXT], BF16)
                v_tm = sb1("v_tm", [128, 9 * D], BF16)
                x_own = sb1("x_own", [128, 2 * D], F32)
                o_sb = sb1("o_sb", [128, 2 * D], BF16)
                oT = sb1("oT", [128, D], BF16)
                attnT = sb1("attnT", [128, 4 * 256], BF16)
                x1b = o_sb
                smx = sb1("smx", [128, 4 * 134], F32)
                a_bf = sb1("a_bf", [128, 4 * 134], BF16)
                nmx = sb1("nmx", [128, 4], F32)

                pt = [ps1("pt0", [128, 512], F32), ps1("pt1", [128, 512], F32)]
                ptb = [ps1("ptb0", [128, 256], BF16), ps1("ptb1", [128, 256], BF16)]
                pj = [ps1("pj0", [128, 512], F32), ps1("pj1", [128, 512], F32),
                      ps1("pj2", [128, 512], F32), ps1("pj3", [128, 512], F32)]

                xld = [None] * 9
                xtr_ev = {}
                xte_ev = {}

                def emit_xld(t):
                    rows = 128 if t < 8 else 6
                    src = xh_d.ap()[B + t * 128: B + t * 128 + rows, :]
                    waits = [xtr_ev[(t - 2, 7)]] if t >= 2 else []
                    xld[t] = p.op("pool",
                                  lambda e, t=t, rows=rows, src=src:
                                  e.dma_start(xb[0:rows, (t % 2) * D:(t % 2) * D + D], src),
                                  waits=waits, inc="sXB%d" % (t % 2), k=16)

                xld[0] = pre[(half, "xld", 0)]
                xld[1] = pre[(half, "xld", 1)]
                xown_ev = [None] * TPH
                for t in range(2):
                    src = xh_d.ap()[B + t * 128 + 3: B + t * 128 + 131, :]
                    xown_ev[t] = p.op("sp",
                                      lambda e, t=t, src=src:
                                      e.dma_start(x_own[:, (t % 2) * D:(t % 2) * D + D], src),
                                      inc="sXO%d" % (t % 2), k=16)

                for t in range(9):
                    rows = 128 if t < 8 else 6
                    for c in range(8):
                        waits = [xld[t], ev_idbf]
                        ev_e = xte_ev.get((t, c - 2)) if c >= 2 else xte_ev.get((t - 1, 6 + c))
                        if ev_e is not None:
                            waits.append(ev_e)
                        xtr_ev[(t, c)] = p.op(
                            "pe",
                            lambda e, t=t, c=c, rows=rows:
                            e.transpose(ptb[c % 2][0:128, 0:rows],
                                        xb[0:rows,
                                           (t % 2) * D + c * 128:(t % 2) * D + c * 128 + 128],
                                        id_bf[0:rows, 0:rows]),
                            waits=waits, inc="sPE")
                        xte_ev[(t, c)] = p.op(
                            "act",
                            lambda e, t=t, c=c, rows=rows:
                            e.copy(xt_ext[:, c * EXT + t * 128: c * EXT + t * 128 + rows],
                                   ptb[c % 2][0:128, 0:rows]),
                            waits=[xtr_ev[(t, c)]], inc="sACT")
                    if t + 2 < 9:
                        emit_xld(t + 2)
                if (half, "wq") in pre:
                    wq_ev = pre[(half, "wq")]
                    wk_ev = pre[(half, "wk")]
                else:
                    wq_ev = p.op("pool", lambda e: e.dma_start(
                        wbuf[0][:].rearrange("p (c n) -> p c n", c=8),
                        wq_d.ap().rearrange("(c p) n -> p c n", p=128)), inc="sWQ0", k=16)
                    wk_ev = p.op("pool", lambda e: e.dma_start(
                        wbuf[1][:].rearrange("p (c n) -> p c n", c=8),
                        wk_d.ap().rearrange("(c p) n -> p c n", p=128)), inc="sWQ1", k=16)
                p.flush("A1_h%d" % half)

                pj_ev = [None, None, None, None]
                pjn = [0]

                def proj_fm(widx, wait_w, oc, tok0, n, dst_ap_fn, bias_col):
                    bank = pjn[0] % 4
                    pjn[0] += 1
                    first_waits = [wait_w, pj_ev[bank], ev_setup]
                    last = None
                    for ic in range(8):
                        last = p.op(
                            "pe",
                            lambda e, ic=ic, bank=bank, widx=widx, oc=oc, tok0=tok0, n=n:
                            e.matmul(pj[bank][:, 0:n],
                                     wbuf[widx][:, ic * D + oc * 128: ic * D + oc * 128 + 128],
                                     xt_ext[:, ic * EXT + tok0: ic * EXT + tok0 + n],
                                     start=(ic == 0), stop=(ic == 7)),
                            waits=first_waits if ic == 0 else (),
                            inc="sPE" if ic == 7 else None)
                    ev = p.op("act",
                              lambda e, bank=bank, n=n, dst_ap_fn=dst_ap_fn, bias_col=bias_col:
                              e.activation(dst_ap_fn(), pj[bank][:, 0:n], AF.Identity,
                                           bias=bqkpc[:, bias_col:bias_col + 1], scale=1.0),
                              waits=[last], inc="sACT")
                    pj_ev[bank] = ev
                    return last, ev

                q_last_mm = None
                for oc in range(8):
                    for ts in range(2):
                        q_last_mm, _ = proj_fm(
                            0, wq_ev, oc, 3 + ts * 512, 512,
                            lambda oc=oc, ts=ts: qT[:, oc * HALF_T + ts * 512:
                                                    oc * HALF_T + ts * 512 + 512],
                            oc)
                k_last_mm = None
                for oc in range(8):
                    for ts, n in ((0, 512), (1, 512), (2, 6)):
                        k_last_mm, _ = proj_fm(
                            1, wk_ev, oc, ts * 512, n,
                            lambda oc=oc, ts=ts, n=n: kT[:, oc * EXT + ts * 512:
                                                         oc * EXT + ts * 512 + n],
                            8 + oc)

                wv_ev = p.op("pool", lambda e: e.dma_start(
                    wbuf[0][:].rearrange("p (c n) -> p c n", c=8),
                    wv_d.ap().rearrange("(c p) n -> p c n", p=128)),
                    waits=[q_last_mm], inc="sWQ0", k=16)
                wo_ev = p.op("pool", lambda e: e.dma_start(
                    wbuf[1][:].rearrange("p (c n) -> p c n", c=8),
                    wo_d.ap().rearrange("(c p) n -> p c n", p=128)),
                    waits=[k_last_mm], inc="sWQ1", k=16)

                for vt in range(5):
                    for fs in range(2):
                        bank = pjn[0] % 4
                        pjn[0] += 1
                        first_waits = [wv_ev, pj_ev[bank]]
                        last = None
                        for ic in range(8):
                            last = p.op(
                                "pe",
                                lambda e, ic=ic, bank=bank, vt=vt, fs=fs:
                                e.matmul(pj[bank][0:128, 0:512],
                                         xt_ext[:, ic * EXT + vt * 128: ic * EXT + vt * 128 + 128],
                                         wbuf[0][:, ic * D + fs * 512: ic * D + fs * 512 + 512],
                                         start=(ic == 0), stop=False),
                                waits=first_waits if ic == 0 else (), inc=None)
                        last = p.op(
                            "pe",
                            lambda e, bank=bank, fs=fs:
                            e.matmul(pj[bank][0:128, 0:512], ones64(128),
                                     bnd[64:65, fs * 512: fs * 512 + 512],
                                     start=False, stop=True),
                            waits=[ev_setup], inc="sPE")
                        ev = p.op("act",
                                  lambda e, bank=bank, vt=vt, fs=fs:
                                  e.copy(v_tm[0:128, vt * D + fs * 512: vt * D + fs * 512 + 512],
                                         pj[bank][0:128, 0:512]),
                                  waits=[last], inc="sACT")
                        pj_ev[bank] = ev
                vrdy = {}
                vchain_bank = [None, None]

                def vdrip(vt, guards):
                    rows = 128 if vt < 8 else 6
                    ev = None
                    for fs in range(2):
                        last = None
                        for ic in range(8):
                            last = p.op(
                                "pe",
                                lambda e, ic=ic, vt=vt, rows=rows, fs=fs:
                                e.matmul(pj[fs][0:rows, 0:512],
                                         xt_ext[:, ic * EXT + vt * 128: ic * EXT + vt * 128 + rows],
                                         wbuf[0][:, ic * D + fs * 512: ic * D + fs * 512 + 512],
                                         start=(ic == 0), stop=False),
                                waits=(guards + [wv_ev]) if ic == 0 else (), inc=None)
                        last = p.op(
                            "pe",
                            lambda e, rows=rows, fs=fs:
                            e.matmul(pj[fs][0:rows, 0:512], ones64(rows),
                                     bnd[64:65, fs * 512: fs * 512 + 512],
                                     start=False, stop=True),
                            waits=(), inc="sPE")
                        ev = p.op("act",
                                  lambda e, vt=vt, rows=rows, fs=fs:
                                  e.copy(v_tm[0:rows, vt * D + fs * 512: vt * D + fs * 512 + 512],
                                         pj[fs][0:rows, 0:512]),
                                  waits=[last], inc="sACT")
                        vchain_bank[fs] = ev
                    vrdy[vt] = ev
                p.flush("A2_h%d" % half)

                if debug and half == 0:
                    for key, src in (("qT", qT[:]), ("kT", kT[:]), ("v", v_tm[:]),
                                     ("xt0", xt_ext[:])):
                        p.op("sp", lambda e, key=key, src=src:
                             e.dma_start(dbg[key].ap(), src), inc="sO0", k=16)
                sc_ev = {}
                sm_ev = {}
                nmx_ev = {}
                exp_ev = {}
                rs_ev = {}
                t_ev = {}
                atc_ev = {}
                av_ev = {}
                oe_ev = {}
                ote_ev = {}
                oproj_ev = {}
                lnr_ev = {}
                x1_ev = [None] * TPH
                xte1_ev = {}
                xtr1_ev = {}
                otr_ev = {}
                x1b_evs = {}
                lgc_ev = [None] * TPH
                pm_ev = [None] * TPH
                msk2_ev = [None] * TPH
                comb_ev = [None] * TPH
                apply1_ev = [None] * TPH

                def emit_score(t, h):
                    ch, pb = h // 2, (h % 2) * 64
                    bk = t % 2
                    sl = (t % 2) * 2 + h % 2
                    hs = (t % 2) * 16 + h
                    moff = 0 if (half == 0 and t == 0) else (268 if (half == 1 and t == 7) else 134)
                    waits = [sm_ev.get((t, h - 1)) if h >= 1 else sm_ev.get((t - 2, 15)),
                             vchain_bank[t % 2], ev_setup]
                    sc_ev[(t, h)] = p.op(
                        "pe",
                        lambda e, t=t, h=h, ch=ch, pb=pb, bk=bk:
                        e.matmul(pj[bk][:, 0:134],
                                 qT[pb:pb + 64, ch * HALF_T + t * 128: ch * HALF_T + t * 128 + 128],
                                 kT[pb:pb + 64, ch * EXT + t * 128: ch * EXT + t * 128 + 134],
                                 start=True, stop=True),
                        waits=waits, inc="sPE")
                    w2_ = [sc_ev[(t, h)],
                           exp_ev.get((t, h - 2)) if h >= 2 else exp_ev.get((t - 2, 14 + h))]
                    sm_ev[(t, h)] = p.op(
                        "dve",
                        lambda e, bk=bk, sl=sl, moff=moff:
                        e.scalar_tensor_tensor(smx[:, sl * 134: sl * 134 + 134],
                                               pj[bk][:, 0:134], consts[:, 2:3],
                                               masks[:, moff:moff + 134],
                                               AluOpType.mult, AluOpType.add),
                        waits=w2_, inc="sDVE")
                    nmx_ev[(t, h)] = p.op(
                        "dve",
                        lambda e, sl=sl:
                        e.reduce_max(nmx[:, sl:sl + 1],
                                     smx[:, sl * 134: sl * 134 + 134],
                                     axis=AX.X, negate=True),
                        waits=[sm_ev[(t, h)]], inc="sDVE")
                    w3_ = [sm_ev[(t, h)], nmx_ev[(t, h)],
                           t_ev.get((t, h - 2)) if h >= 2 else t_ev.get((t - 2, 14 + h)),
                           rs_ev.get((t - 2, 15))]
                    exp_ev[(t, h)] = p.op(
                        "act",
                        lambda e, sl=sl, hs=hs:
                        e.activation(a_bf[:, sl * 134: sl * 134 + 134],
                                     smx[:, sl * 134: sl * 134 + 134],
                                     AF.Exp, bias=nmx[:, sl:sl + 1], scale=1.0,
                                     accum_out=St[:, hs:hs + 1]),
                        waits=w3_, inc="sACT")

                def emit_tav(t, h):
                    bk = t % 2
                    sl = (t % 2) * 2 + h % 2
                    hs = (t % 2) * 16 + h
                    at = (t % 2) * 512 + (h % 2) * 256
                    prev1 = (t // 2) * 2 - 1
                    rs_ev[(t, h)] = p.op(
                        "dve",
                        lambda e, hs=hs: e.reciprocal(rSt[:, hs:hs + 1], St[:, hs:hs + 1]),
                        waits=[exp_ev[(t, h)], oe_ev.get((t - 2, 15))], inc="sDVE")
                    wt = [exp_ev[(t, h)],
                          atc_ev.get((t, h - 1)) if h >= 1 else xte1_ev.get((prev1, 6 + t % 2))]
                    p.op("pe",
                         lambda e, bk=bk, sl=sl:
                         e.transpose(ptb[bk][:, 0:128],
                                     a_bf[:, sl * 134: sl * 134 + 128], id_bf[:]),
                         waits=wt, inc=None)
                    t_ev[(t, h)] = p.op(
                        "pe",
                        lambda e, bk=bk, sl=sl:
                        e.transpose(ptb[bk][0:6, 128:256],
                                    a_bf[:, sl * 134 + 128: sl * 134 + 134], id_bf[:]),
                        waits=(), inc="sPE")
                    wc = [t_ev[(t, h)],
                          av_ev.get((t, h - 2)) if h >= 2 else av_ev.get((t - 2, 14 + h))]
                    p.op("dve",
                         lambda e, bk=bk, at=at:
                         e.tensor_copy(attnT[:, at: at + 128], ptb[bk][:, 0:128]),
                         waits=wc, inc=None)
                    atc_ev[(t, h)] = p.op(
                        "dve",
                        lambda e, bk=bk, at=at:
                        e.tensor_copy(attnT[0:6, at + 128: at + 256], ptb[bk][0:6, 128:256]),
                        waits=(), inc="sDVE")
                    wav = [atc_ev[(t, h)],
                           oe_ev.get((t, h - 1)) if h >= 1 else oe_ev.get((t - 2, 15)),
                           vrdy.get(t), vrdy.get(t + 1)]
                    p.op("pe",
                         lambda e, t=t, h=h, bk=bk, at=at:
                         e.matmul(pj[2 + bk][:, 0:64],
                                  attnT[:, at: at + 128],
                                  v_tm[:, t * D + h * 64: t * D + h * 64 + 64],
                                  start=True, stop=False),
                         waits=wav, inc=None)
                    av_ev[(t, h)] = p.op(
                        "pe",
                        lambda e, t=t, h=h, bk=bk, at=at:
                        e.matmul(pj[2 + bk][:, 0:64],
                                 attnT[0:6, at + 128: at + 256],
                                 v_tm[0:6, (t + 1) * D + h * 64:(t + 1) * D + h * 64 + 64],
                                 start=False, stop=True),
                        waits=(), inc="sPE")
                    oe_ev[(t, h)] = p.op(
                        "dve",
                        lambda e, t=t, h=h, bk=bk, hs=hs:
                        e.tensor_single_scalar(
                            o_sb[:, (t % 2) * D + h * 64:(t % 2) * D + h * 64 + 64],
                            pj[2 + bk][:, 0:64], rSt[:, hs:hs + 1], AluOpType.mult),
                        waits=[av_ev[(t, h)], rs_ev[(t, h)]], inc="sDVE")

                def emit_oT(t):
                    pb0 = (t // 2) * 2
                    for c in range(8):
                        wt = [oe_ev[(t, NH - 1)], atc_ev[(pb0 + c % 2, NH - 1)]]
                        if c >= 2:
                            wt.append(ote_ev[(t, c - 2)])
                        elif t >= 1:
                            wt.append(xte1_ev.get((t - 1, 6 + c)))
                        if t >= 1:
                            wt.append(oproj_ev.get((t - 1, 1)))
                        otr_ev[(t, c)] = ev = p.op("pe",
                                  lambda e, t=t, c=c:
                                  e.transpose(ptb[c % 2][:, 0:128],
                                              o_sb[:, (t % 2) * D + c * 128:(t % 2) * D + c * 128 + 128],
                                              id_bf[:]),
                                  waits=wt, inc="sPE")
                        ote_ev[(t, c)] = p.op(
                            "dve",
                            lambda e, t=t, c=c:
                            e.tensor_copy(oT[:, c * 128: c * 128 + 128],
                                          ptb[c % 2][:, 0:128]),
                            waits=[ev, oproj_ev.get((t - 1, 1))], inc="sDVE")

                def emit_pair_heads(t0, t1):
                    emit_score(t0, 0)
                    emit_score(t1, 0)
                    for h in range(NH):
                        if h + 1 < NH:
                            emit_score(t0, h + 1)
                            emit_score(t1, h + 1)
                        emit_tav(t0, h)
                        emit_tav(t1, h)

                for pr in range(TPH // 2):
                  t0_, t1_ = 2 * pr, 2 * pr + 1
                  emit_pair_heads(t0_, t1_)
                  if debug and half == 0 and pr == 0:
                      p.op("sp", lambda e: e.dma_start(dbg["o0"].ap(), o_sb[:, 0:D]),
                           waits=[oe_ev[(0, NH - 1)]], inc="sO0", k=16)
                  for t in (t0_, t1_):
                    emit_oT(t)

                    for fs in range(2):
                        wts = [ote_ev[(t, 7)], wo_ev]
                        if t >= 1:
                            wts += [lgc_ev[t - 1], lnr_ev[(t - 1, fs)]]
                        last = None
                        for ic in range(8):
                            last = p.op(
                                "pe",
                                lambda e, ic=ic, fs=fs:
                                e.matmul(pt[fs][:, 0:512],
                                         oT[:, ic * 128: ic * 128 + 128],
                                         wbuf[1][:, ic * D + fs * 512: ic * D + fs * 512 + 512],
                                         start=(ic == 0), stop=False),
                                waits=wts if ic == 0 else (), inc=None)
                        oproj_ev[(t, fs)] = p.op(
                            "pe",
                            lambda e, fs=fs:
                            e.matmul(pt[fs][:, 0:512], ones0(128),
                                     bnd[0:1, 2 * D + fs * 512: 2 * D + fs * 512 + 512],
                                     start=False, stop=True),
                            waits=(), inc="sPE")

                    if 2 <= t <= 5:
                        vdrip(t + 3, [sm_ev[(t0_, 15)], sm_ev[(t1_, 15)]])

                    for fs in range(2):
                        wl = [oproj_ev[(t, fs)], xown_ev[t]]
                        if t >= 2:
                            wl.append(apply1_ev[t - 2])
                            wl.append(x1b_evs.get(t - 2))
                        lnr_ev[(t, fs)] = p.op(
                            "dve",
                            lambda e, t=t, fs=fs, sp8=(t % 2) * 8:
                            e.scalar_tensor_tensor(
                                r2[:, (t % 2) * D + fs * 512:(t % 2) * D + fs * 512 + 512],
                                pt[fs][:, 0:512], 0.0,
                                x_own[:, (t % 2) * D + fs * 512:(t % 2) * D + fs * 512 + 512],
                                AluOpType.add, AluOpType.add,
                                accum_out=scr[:, sp8 + fs:sp8 + fs + 1]),
                            waits=wl, inc="sDVE")
                    if t + 2 < TPH:
                        src = xh_d.ap()[B + (t + 2) * 128 + 3: B + (t + 2) * 128 + 131, :]
                        xown_ev[t + 2] = p.op(
                            "sp",
                            lambda e, t=t, src=src:
                            e.dma_start(x_own[:, (t % 2) * D:(t % 2) * D + D], src),
                            waits=[lnr_ev[(t, 0)], lnr_ev[(t, 1)]],
                            inc="sXO%d" % (t % 2), k=16)
                    sp8 = (t % 2) * 8
                    s1_ev = p.op("dve",
                                 lambda e, sp8=sp8: e.tensor_add(scr[:, sp8 + 2:sp8 + 3],
                                                                 scr[:, sp8 + 0:sp8 + 1],
                                                                 scr[:, sp8 + 1:sp8 + 2]),
                                 waits=[lnr_ev[(t, 0)], lnr_ev[(t, 1)]], inc="sDVE")
                    nm_ev = p.op("dve",
                                 lambda e, sp8=sp8:
                                 e.tensor_single_scalar(scr[:, sp8 + 3:sp8 + 4],
                                                        scr[:, sp8 + 2:sp8 + 3],
                                                        consts[:, 3:4], AluOpType.mult),
                                 waits=[s1_ev], inc="sDVE")
                    cen_ev = p.op(
                        "dve",
                        lambda e, t=t, sp8=sp8:
                        e.tensor_single_scalar(r2[:, (t % 2) * D:(t % 2) * D + D],
                                               r2[:, (t % 2) * D:(t % 2) * D + D],
                                               scr[:, sp8 + 3:sp8 + 4], AluOpType.add),
                        waits=[nm_ev], inc="sDVE")
                    sq_w = [cen_ev]
                    if store_ev[(half * 8 + t) - 2] is not None and stage == "full":
                        sq_w.append(store_ev[(half * 8 + t) - 2])
                    sq_ev = p.op(
                        "act",
                        lambda e, t=t, sp8=sp8:
                        e.activation(ostg[:, (t % 2) * D:(t % 2) * D + D],
                                     r2[:, (t % 2) * D:(t % 2) * D + D],
                                     AF.Square, accum_out=scr[:, sp8 + 4:sp8 + 5]),
                        waits=sq_w, inc="sACT")
                    ln_ev = p.op("act",
                         lambda e, sp8=sp8: e.activation(scr[:, sp8 + 5:sp8 + 6],
                                                         scr[:, sp8 + 4:sp8 + 5], AF.Ln,
                                                         scale=1.0 / D, bias=eps_ap),
                         waits=[sq_ev], inc="sACT")
                    rstd_ev = p.op("act",
                         lambda e, sp8=sp8: e.activation(scr[:, sp8 + 6:sp8 + 7],
                                                         scr[:, sp8 + 5:sp8 + 6],
                                                         AF.Exp, scale=-0.5),
                         waits=[ln_ev], inc="sACT")
                    apply1_ev[t] = p.op(
                        "act",
                        lambda e, t=t, sp8=sp8:
                        e.mul(x1f[:, t * D: t * D + D], r2[:, (t % 2) * D:(t % 2) * D + D],
                              scr[:, sp8 + 6:sp8 + 7]),
                        waits=[rstd_ev], inc="sACT")

                    if debug and half == 0 and t == 0:
                        p.op("sp", lambda e: e.dma_start(dbg["scr0"].ap(), scr[:]),
                             waits=[apply1_ev[0]], inc="sO0", k=16)
                        p.op("sp", lambda e: e.dma_start(dbg["r20"].ap(), r2[:, 0:D]),
                             waits=[apply1_ev[0]], inc="sO0", k=16)
                        p.op("sp", lambda e: e.dma_start(dbg["oT0"].ap(), oT[:]),
                             waits=[oproj_ev[(0, 1)]], inc="sO0", k=16)

                    wcp = [apply1_ev[t], otr_ev[(t, 7)], xtr1_ev.get((t - 2, 7))]
                    x1b_a = p.op("pool",
                                 lambda e, t=t:
                                 e.tensor_copy(x1b[:, (t % 2) * D:(t % 2) * D + 512],
                                               x1f[:, t * D: t * D + 512]),
                                 waits=wcp, inc="sPL")
                    x1b_ev = p.op("pool",
                                  lambda e, t=t:
                                  e.tensor_copy(x1b[:, (t % 2) * D + 512:(t % 2) * D + D],
                                                x1f[:, t * D + 512: t * D + D]),
                                  waits=(), inc="sPL")
                    x1b_evs[t] = x1b_ev
                    for c in range(8):
                        wt = [x1b_a if c < 4 else x1b_ev]
                        if c >= 2:
                            wt.append(xte1_ev[(t, c - 2)])
                        else:
                            wt.append(ote_ev[(t, 6 + c)])
                        xtr1_ev[(t, c)] = p.op(
                            "pe",
                            lambda e, t=t, c=c:
                            e.transpose(ptb[c % 2][:, 0:128],
                                        x1b[:, (t % 2) * D + c * 128:(t % 2) * D + c * 128 + 128],
                                        id_bf[:]),
                            waits=wt, inc="sPE")
                        xte1_ev[(t, c)] = p.op(
                            "act",
                            lambda e, t=t, c=c:
                            e.copy(xt1[:, c * HALF_T + t * 128: c * HALF_T + t * 128 + 128],
                                   ptb[c % 2][:, 0:128]),
                            waits=[xtr1_ev[(t, c)]], inc="sACT")

                    glast = None
                    for ic in range(8):
                        glast = p.op(
                            "pe",
                            lambda e, t=t, ic=ic:
                            e.matmul(pt[0][:, 0:8],
                                     xt1[:, ic * HALF_T + t * 128: ic * HALF_T + t * 128 + 128],
                                     gwb[:, ic * 8: ic * 8 + 8],
                                     start=(ic == 0), stop=False),
                            waits=[xte1_ev[(t, 7)], ev_gwb] if ic == 0 else (), inc=None)
                    glast = p.op("pe",
                                 lambda e: e.matmul(pt[0][:, 0:8], ones32(128),
                                                    bnd[32:33, 3 * D: 3 * D + 8],
                                                    start=False, stop=True),
                                 waits=(), inc="sPE")
                    wlg = [glast]
                    if t >= 1:
                        wlg += [msk2_ev[t - 1], pm_ev[t - 1]]
                    lgc_ev[t] = p.op("act",
                                     lambda e: e.copy(smb[:, 0:8], pt[0][:, 0:8]),
                                     waits=wlg, inc="sACT")
                    m1_ev = p.op("dve",
                                 lambda e: e.reduce_max(smb[:, 40:41], smb[:, 0:8], axis=AX.X),
                                 waits=[lgc_ev[t]], inc="sDVE")
                    nm1_ev = p.op("dve",
                                  lambda e: e.reduce_max(smb[:, 41:42], smb[:, 0:8],
                                                         axis=AX.X, negate=True),
                                  waits=(), inc="sDVE")
                    msk_ev = p.op("dve",
                                  lambda e: e.tensor_single_scalar(smb[:, 8:16], smb[:, 0:8],
                                                                   smb[:, 40:41], AluOpType.is_ge),
                                  waits=[m1_ev], inc="sDVE")
                    l2_ev = p.op("dve",
                                 lambda e: e.scalar_tensor_tensor(smb[:, 8:16], smb[:, 8:16],
                                                                  neg_ap, smb[:, 0:8],
                                                                  AluOpType.mult, AluOpType.add),
                                 waits=[ev_setup, msk_ev], inc="sDVE")
                    m2_ev = p.op("dve",
                                 lambda e: e.reduce_max(smb[:, 42:43], smb[:, 8:16], axis=AX.X),
                                 waits=[l2_ev], inc="sDVE")
                    wpe = [nm1_ev, lgc_ev[t]]
                    if t >= 1:
                        wpe.append(pm_ev[t - 1])
                    pexp_ev = p.op("act",
                                   lambda e: e.activation(smb[:, 16:24], smb[:, 0:8], AF.Exp,
                                                          bias=smb[:, 41:42], scale=1.0),
                                   waits=wpe, inc="sACT")
                    msk2_ev[t] = p.op("dve",
                                      lambda e: e.tensor_single_scalar(smb[:, 24:32], smb[:, 0:8],
                                                                       smb[:, 42:43],
                                                                       AluOpType.is_ge),
                                      waits=[m2_ev], inc="sDVE")
                    pm_ev[t] = p.op("dve",
                                    lambda e: e.tensor_mul(smb[:, 32:40], smb[:, 16:24],
                                                           smb[:, 24:32]),
                                    waits=[pexp_ev, msk2_ev[t]], inc="sDVE")
                    z_ev = p.op("dve",
                                lambda e: e.reduce_sum(smb[:, 43:44], smb[:, 32:40], axis=AX.X),
                                waits=[pm_ev[t]], inc="sDVE")
                    rz_ev = p.op("dve",
                                 lambda e: e.reciprocal(smb[:, 44:45], smb[:, 43:44]),
                                 waits=[z_ev], inc="sDVE")
                    comb_ev[t] = p.op(
                        "dve",
                        lambda e, t=t:
                        e.tensor_single_scalar(comb[:, t * 8: t * 8 + 8], smb[:, 32:40],
                                               smb[:, 44:45], AluOpType.mult),
                        waits=[rz_ev], inc="sDVE")
                p.flush("A3_h%d" % half)

            if stage == "ln1":
                if debug and half == 0:
                    p.op("sp", lambda e: e.dma_start(dbg["x1f"].ap(), x1f[:]),
                         inc="sO0", k=16)
                for t in range(TPH):
                    g = half * 8 + t
                    store_ev[g] = p.op(
                        "sp",
                        lambda e, t=t:
                        e.dma_start(out_d.ap()[half * HALF_T + t * 128:
                                               half * HALF_T + t * 128 + 128, :],
                                    x1f[:, t * D: t * D + D]),
                        waits=[apply1_ev[t]], inc="sO%d" % (t % 2), k=16)
                p.flush("ST_h%d" % half)
                continue

            with ExitStack() as S2:
                sb2 = lambda n, s, d: S2.enter_context(nc.sbuf_tensor("m%d_" % half + n, s, d))
                ps2 = lambda n, s, d: S2.enter_context(nc.psum_tensor("m%d_" % half + n, s, d))
                w1b = [sb2("w1b0", [128, NJ * GCOLS], BF16)]
                w2b = [sb2("w2b0", [128, NJ * GCOLS], BF16)]
                hg = sb2("hg", [128, NJ * HALF_T], BF16)
                cbT = sb2("cbT", [40, D], F32)
                acc = sb2("acc", [128, TPH * D], F32)
                pH = [ps2("pH0", [128, 512], F32), ps2("pH1", [128, 512], F32)]
                pY = [[ps2("pY00", [128, 512], F32), ps2("pY01", [128, 512], F32)],
                      [ps2("pY10", [128, 512], F32), ps2("pY11", [128, 512], F32)]]

                ctb_ev = {}
                accin_ev = {}
                b2mm_ev = {}
                for t in range(TPH):
                    wt = [comb_ev[t]]
                    if t >= 2:
                        wt.append(ctb_ev[t - 2])
                    ev = p.op("pe",
                              lambda e, t=t:
                              e.transpose(pH[t % 2][0:8, 0:128], comb[:, t * 8: t * 8 + 8],
                                          id_f[:]),
                              waits=wt, inc="sPE")
                    ctb_ev[t] = p.op(
                        "act",
                        lambda e, t=t:
                        e.copy(cbT[32:40, t * 128: t * 128 + 128], pH[t % 2][0:8, 0:128]),
                        waits=[ev], inc="sACT")
                    for fs in range(2):
                        wb = [ctb_ev[t]]
                        if t >= 2:
                            wb.append(accin_ev[(t - 2, fs)])
                        b2mm_ev[(t, fs)] = p.op(
                            "pe",
                            lambda e, t=t, fs=fs:
                            e.matmul(pY[t % 2][fs][:, 0:512],
                                     cbT[32:40, t * 128: t * 128 + 128],
                                     b2sb[32:40, fs * 512: fs * 512 + 512],
                                     start=True, stop=True),
                            waits=wb, inc="sPE")
                        accin_ev[(t, fs)] = p.op(
                            "act",
                            lambda e, t=t, fs=fs:
                            e.copy(acc[:, t * D + fs * 512: t * D + fs * 512 + 512],
                                   pY[t % 2][fs][:, 0:512]),
                            waits=[b2mm_ev[(t, fs)]], inc="sACT")

                apply2_ev = [None] * TPH

                def emit_l2(t):
                    g = half * 8 + t
                    wr = [stt_ev[(31, t, 0)], stt_ev[(31, t, 1)]]
                    if t >= 2:
                        wr.append(apply2_ev[t - 2])
                    sp8 = (t % 2) * 8
                    r2s_ev = p.op(
                        "dve",
                        lambda e, t=t, sp8=sp8:
                        e.scalar_tensor_tensor(r2[:, (t % 2) * D:(t % 2) * D + D],
                                               x1f[:, t * D: t * D + D], 0.0,
                                               acc[:, t * D: t * D + D],
                                               AluOpType.add, AluOpType.add,
                                               accum_out=scr[:, sp8 + 2:sp8 + 3]),
                        waits=wr, inc="sDVE")
                    nm_ev = p.op("dve",
                                 lambda e, sp8=sp8:
                                 e.tensor_single_scalar(scr[:, sp8 + 3:sp8 + 4],
                                                        scr[:, sp8 + 2:sp8 + 3],
                                                        consts[:, 3:4], AluOpType.mult),
                                 waits=[r2s_ev], inc="sDVE")
                    cen_ev = p.op(
                        "dve",
                        lambda e, t=t, sp8=sp8:
                        e.tensor_single_scalar(r2[:, (t % 2) * D:(t % 2) * D + D],
                                               r2[:, (t % 2) * D:(t % 2) * D + D],
                                               scr[:, sp8 + 3:sp8 + 4], AluOpType.add),
                        waits=[nm_ev], inc="sDVE")
                    sq_w = [cen_ev]
                    if g >= 2 and store_ev[g - 2] is not None:
                        sq_w.append(store_ev[g - 2])
                    sq2_ev = p.op("act",
                         lambda e, t=t, sp8=sp8:
                         e.activation(ostg[:, (t % 2) * D:(t % 2) * D + D],
                                      r2[:, (t % 2) * D:(t % 2) * D + D],
                                      AF.Square, accum_out=scr[:, sp8 + 4:sp8 + 5]),
                         waits=sq_w, inc="sACT")
                    ln2_ev = p.op("act",
                         lambda e, sp8=sp8: e.activation(scr[:, sp8 + 5:sp8 + 6],
                                                         scr[:, sp8 + 4:sp8 + 5], AF.Ln,
                                                         scale=1.0 / D, bias=eps_ap),
                         waits=[sq2_ev], inc="sACT")
                    rstd2_ev = p.op("act",
                         lambda e, sp8=sp8: e.activation(scr[:, sp8 + 6:sp8 + 7],
                                                         scr[:, sp8 + 5:sp8 + 6],
                                                         AF.Exp, scale=-0.5),
                         waits=[ln2_ev], inc="sACT")
                    apply2_ev[t] = p.op(
                        "act",
                        lambda e, t=t, sp8=sp8:
                        e.mul(ostg[:, (t % 2) * D:(t % 2) * D + D],
                              r2[:, (t % 2) * D:(t % 2) * D + D], scr[:, sp8 + 6:sp8 + 7]),
                        waits=[rstd2_ev], inc="sACT")
                    store_ev[g] = p.op(
                        "sp",
                        lambda e, t=t, half=half:
                        e.dma_start(out_d.ap()[half * HALF_T + t * 128:
                                               half * HALF_T + t * 128 + 128, :],
                                    ostg[:, (t % 2) * D:(t % 2) * D + D]),
                        waits=[apply2_ev[t]], inc="sO%d" % (t % 2), k=16)

                h_ev = {}
                hrelu_ev = {}
                y_ev = {}
                stt_ev = {}
                w1_ev = {}
                w2_ev = {}
                for e_i in range(NE):
                    for g_i in range(NG):
                        step = e_i * NG + g_i
                        par = step % 2
                        wl = []
                        if step >= 1:
                            wl.append(h_ev[(step - 1, 15)])
                        elif moe_last["mm1"] is not None:
                            wl.append(moe_last["mm1"])
                        w1_ev[step] = p.op(
                            "pool",
                            lambda e, e_i=e_i, g_i=g_i:
                            e.dma_start(
                                w1b[0][:].rearrange("p (c n) -> p c n", c=8),
                                w1_d.ap()[e_i, :, g_i * GCOLS:(g_i + 1) * GCOLS]
                                .rearrange("(c p) n -> p c n", p=128)),
                            waits=wl, inc="sW1%d" % (step % 2), k=16)
                        wl = []
                        if step >= 1:
                            wl.append(y_ev[(step - 1, 7, 1)])
                        elif moe_last["mm2"] is not None:
                            wl.append(moe_last["mm2"])
                        w2_ev[step] = p.op(
                            "pool",
                            lambda e, e_i=e_i, g_i=g_i:
                            e.dma_start(
                                w2b[0][:].rearrange("p (c n) -> p c n", c=8),
                                w2_d.ap()[e_i, g_i * GCOLS:(g_i + 1) * GCOLS, :]
                                .rearrange("(c p) n -> p c n", p=128)),
                            waits=wl, inc="sW2%d" % (step % 2), k=16)

                        for j in range(NJ):
                            for ts in range(2):
                                jj = j * 2 + ts
                                wm = [w1_ev[step]]
                                if jj >= 2:
                                    wm.append(hrelu_ev[(step, jj - 2)])
                                elif step >= 1:
                                    wm.append(hrelu_ev[(step - 1, 14 + jj)])
                                else:
                                    wm.append(ctb_ev[6 + jj])
                                    wm.append(moe_last["hrelu14" if jj == 0 else "hrelu15"])
                                last = None
                                for ic in range(8):
                                    last = p.op(
                                        "pe",
                                        lambda e, ic=ic, j=j, ts=ts, par=par, jj=jj:
                                        e.matmul(pH[jj % 2][:, 0:512],
                                                 w1b[0][:, ic * GCOLS + j * 128:
                                                        ic * GCOLS + j * 128 + 128],
                                                 xt1[:, ic * HALF_T + ts * 512:
                                                     ic * HALF_T + ts * 512 + 512],
                                                 start=(ic == 0), stop=(ic == 7)),
                                        waits=wm if ic == 0 else (),
                                        inc="sPE" if ic == 7 else None)
                                h_ev[(step, jj)] = last
                                wh = [last]
                                if step >= 1:
                                    wh.append(y_ev[(step - 1, 7, 1)])
                                elif moe_last["mm2"] is not None:
                                    wh.append(moe_last["mm2"])
                                hrelu_ev[(step, jj)] = p.op(
                                    "act",
                                    lambda e, j=j, ts=ts, jj=jj, e_i=e_i, g_i=g_i:
                                    e.activation(hg[:, j * HALF_T + ts * 512:
                                                    j * HALF_T + ts * 512 + 512],
                                                 pH[jj % 2][:, 0:512], AF.Relu,
                                                 bias=b1pc[:, e_i * 32 + g_i * 8 + j:
                                                           e_i * 32 + g_i * 8 + j + 1],
                                                 scale=1.0),
                                    waits=wh, inc="sACT")

                        for t in range(TPH):
                            wm = [w2_ev[step], hrelu_ev[(step, 15)]]
                            if t >= 2:
                                wm += [stt_ev[(step, t - 2, 0)], stt_ev[(step, t - 2, 1)]]
                            elif step >= 1:
                                wm += [stt_ev[(step - 1, 6 + t, 0)], stt_ev[(step - 1, 6 + t, 1)]]
                            else:
                                wm += [moe_last["stt1" + str(t)] or ev_setup]
                            if step == 0:
                                wm += [accin_ev[(6 + t % 2, 0)], accin_ev[(6 + t % 2, 1)]]
                            for j in range(NJ):
                                for fs in range(2):
                                    last = p.op(
                                        "pe",
                                        lambda e, t=t, j=j, fs=fs, par=par:
                                        e.matmul(pY[t % 2][fs][:, 0:512],
                                                 hg[:, j * HALF_T + t * 128:
                                                    j * HALF_T + t * 128 + 128],
                                                 w2b[0][:, j * GCOLS + fs * 512:
                                                        j * GCOLS + fs * 512 + 512],
                                                 start=(j == 0), stop=(j == NJ - 1)),
                                        waits=wm if (j == 0 and fs == 0) else (),
                                        inc="sPE" if j == NJ - 1 else None)
                                    if j == NJ - 1:
                                        y_ev[(step, t, fs)] = last
                            for fs in range(2):
                                ws = [y_ev[(step, t, fs)], accin_ev[(t, fs)], comb_ev[t]]
                                stt_ev[(step, t, fs)] = p.op(
                                    "dve",
                                    lambda e, t=t, fs=fs, e_i=e_i:
                                    e.scalar_tensor_tensor(
                                        acc[:, t * D + fs * 512: t * D + fs * 512 + 512],
                                        pY[t % 2][fs][:, 0:512],
                                        comb[:, t * 8 + e_i: t * 8 + e_i + 1],
                                        acc[:, t * D + fs * 512: t * D + fs * 512 + 512],
                                        AluOpType.mult, AluOpType.add),
                                    waits=ws, inc="sDVE")
                            if step == 31:
                                emit_l2(t)
                if half == 0:
                    emit_prefetch(1)
                moe_last["mm1"] = h_ev[(31, 15)]
                moe_last["mm2"] = y_ev[(31, 7, 1)]
                moe_last["hrelu14"] = hrelu_ev[(31, 14)]
                moe_last["hrelu15"] = hrelu_ev[(31, 15)]
                moe_last["stt10"] = stt_ev[(31, 6, 1)]
                moe_last["stt11"] = stt_ev[(31, 7, 1)]
                p.flush("ML2_h%d" % half)

        p.op("sp", lambda e: e.nop(),
             waits=[("sO0", p.v["sO0"]), ("sO1", p.v["sO1"])])
        p.flush("fin")
    return nc


def _make_masks(chunk):
    i = np.arange(128)[:, None]
    w = np.arange(134)[None, :]
    band = np.where((w - i >= 0) & (w - i <= 6), 0.0, NEG).astype(np.float32)
    mF = np.where(w >= 3, band, NEG).astype(np.float32) if chunk == 0 else band
    mL = np.where(w <= 130, band, NEG).astype(np.float32) if chunk == 3 else band
    return np.concatenate([mF, band, mL], axis=1)


def kernel(L0_attn_hidden_states, L0_attn_w_q, L0_attn_b_q, L0_attn_w_k, L0_attn_b_k,
           L0_attn_w_v, L0_attn_b_v, L0_attn_w_o, L0_attn_b_o, L0_ffn_gate_w,
           L0_ffn_gate_b, L0_ffn_w1, L0_ffn_b1, L0_ffn_w2, L0_ffn_b2,
           _stage="full"):
    f32 = np.float32
    x = np.ascontiguousarray(np.asarray(L0_attn_hidden_states, dtype=f32)).reshape(16384, D)

    bnd = np.zeros((65, 3 * D + 16), dtype=f32)
    bnd[0, 0:D] = np.asarray(L0_attn_b_q, dtype=f32)
    bnd[32, 0:D] = np.asarray(L0_attn_b_k, dtype=f32)
    bnd[64, 0:D] = np.asarray(L0_attn_b_v, dtype=f32)
    bnd[0, D:2 * D] = 1.0
    bnd[32, D:2 * D] = 1.0
    bnd[64, D:2 * D] = 1.0
    bnd[0, 2 * D:3 * D] = np.asarray(L0_attn_b_o, dtype=f32)
    bnd[32, 3 * D:3 * D + 8] = np.asarray(L0_ffn_gate_b, dtype=f32)
    gw_r = np.ascontiguousarray(
        np.asarray(L0_ffn_gate_w, dtype=f32).reshape(8, 128, 8).transpose(1, 0, 2).reshape(128, 64))
    b1_pc = np.ascontiguousarray(
        np.asarray(L0_ffn_b1, dtype=f32).reshape(8, 32, 128).transpose(2, 0, 1).reshape(128, 256))
    consts = np.zeros((128, 4), dtype=f32)
    consts[:, 0] = 1e-5
    consts[:, 1] = NEG
    consts[:, 2] = 0.125
    consts[:, 3] = -1.0 / 1024.0
    bqk_pc = np.zeros((128, 16), dtype=f32)
    bqk_pc[:, 0:8] = np.asarray(L0_attn_b_q, dtype=f32).reshape(8, 128).T
    bqk_pc[:, 8:16] = np.asarray(L0_attn_b_k, dtype=f32).reshape(8, 128).T
    ident = np.eye(128, dtype=f32)

    shared = {
        "wq": np.ascontiguousarray(np.asarray(L0_attn_w_q, dtype=f32)),
        "wk": np.ascontiguousarray(np.asarray(L0_attn_w_k, dtype=f32)),
        "wv": np.ascontiguousarray(np.asarray(L0_attn_w_v, dtype=f32)),
        "wo": np.ascontiguousarray(np.asarray(L0_attn_w_o, dtype=f32)),
        "w1": np.ascontiguousarray(np.asarray(L0_ffn_w1, dtype=f32)),
        "w2": np.ascontiguousarray(np.asarray(L0_ffn_w2, dtype=f32)),
        "gw_r": gw_r, "b1_pc": b1_pc, "bqk_pc": bqk_pc,
        "b2": np.ascontiguousarray(np.asarray(L0_ffn_b2, dtype=f32)),
        "bias_nd": bnd, "consts": consts, "ident": ident,
    }

    in_maps = []
    for c in range(8):
        batch, chunk = c // 4, c % 4
        r0 = batch * SEQ + chunk * CORE_T
        xh = np.zeros((CORE_T + 6, D), dtype=f32)
        lo = max(r0 - 3, batch * SEQ)
        hi = min(r0 + CORE_T + 3, (batch + 1) * SEQ)
        xh[lo - (r0 - 3): hi - (r0 - 3), :] = x[lo:hi, :]
        m = dict(shared)
        m["xh"] = xh
        m["masks"] = _make_masks(chunk)
        in_maps.append(m)

    nc = build_nc(stage=_stage)
    res = run_bass_kernel_spmd(nc, in_maps, core_ids=list(range(8)))
    out = np.empty((16384, D), dtype=f32)
    for c in range(8):
        out[c * CORE_T:(c + 1) * CORE_T, :] = res.results[c]["out"]
    return out.reshape(2, SEQ, D)


# version 20
# speedup vs baseline: 1.2155x; 1.2155x over previous
import numpy as np
from contextlib import ExitStack

import concourse.bass as bass
import concourse.mybir as mybir
from concourse.bass_utils import run_bass_kernel_spmd
from concourse.alu_op_type import AluOpType

F32 = mybir.dt.float32
BF16 = mybir.dt.bfloat16
AF = mybir.ActivationFunctionType
AX = mybir.AxisListType

SEQ = 8192
D = 1024
NH = 16
CORE_T = 2048
HALF_T = 1024
TPH = 8
EXT = HALF_T + 6
NE = 8
HF = 4096
NG = 4
GCOLS = 1024
NJ = 8
NEG = -1.0e30

ENGS = ("pe", "act", "dve", "pool", "sp")
_DEC = {"pe": "tensor", "act": "scalar", "dve": "vector", "pool": "gpsimd", "sp": "sync"}
DMA_SEMS = ("sSET", "sSETp", "sXB0", "sXB1", "sWQ0", "sWQ1", "sXO0", "sXO1",
             "sW10", "sW11", "sW20", "sW21", "sO0", "sO1")


class Prog:

    def __init__(self, nc, stack):
        self.nc = nc
        names = list(DMA_SEMS) + ["sPE", "sACT", "sDVE", "sPL"]
        self.h = {n: stack.enter_context(nc.semaphore(n)) for n in names}
        self.v = {n: 0 for n in names}
        self.q = {e: [] for e in ENGS}
        self.waited = {e: {n: 0 for n in names} for e in ENGS}
        self.floor = {n: 0 for n in names}

    def op(self, eng, fn, waits=(), inc=None, k=1):
        ev = None
        if inc is not None:
            self.v[inc] += k
            ev = (inc, self.v[inc])
        ws = tuple(w for w in waits if w is not None)
        self.q[eng].append((fn, ws, inc, k))
        return ev

    def flush(self, name):
        q = self.q
        self.q = {e: [] for e in ENGS}
        with self.nc.Block(name) as blk:
            for eng in ENGS:
                ops = q[eng]
                if not ops:
                    continue

                def body(engobj, ops=ops, eng=eng):
                    for fn, waits, incname, k in ops:
                        for (sn, val) in waits:
                            if val <= self.floor[sn]:
                                continue
                            if self.waited[eng][sn] >= val:
                                continue
                            engobj.wait_ge(self.h[sn], val)
                            self.waited[eng][sn] = val
                        inst = fn(engobj)
                        if incname is not None:
                            inst.then_inc(self.h[incname], k)

                getattr(blk, _DEC[eng])(body)
        for n in self.v:
            if n not in DMA_SEMS:
                self.floor[n] = self.v[n]


def build_nc(stage="full", debug=False):
    nc = bass.Bass("TRN2", target_bir_lowering=False)
    dbg = {}
    if debug:
        dbg["qT"] = nc.dram_tensor("dbg_qT", [128, 8 * HALF_T], BF16, kind="ExternalOutput")
        dbg["kT"] = nc.dram_tensor("dbg_kT", [128, 8 * EXT], BF16, kind="ExternalOutput")
        dbg["v"] = nc.dram_tensor("dbg_v", [128, 9 * D], BF16, kind="ExternalOutput")
        dbg["o0"] = nc.dram_tensor("dbg_o0", [128, D], BF16, kind="ExternalOutput")
        dbg["xt0"] = nc.dram_tensor("dbg_xt0", [128, 8 * EXT], BF16, kind="ExternalOutput")
        dbg["scr0"] = nc.dram_tensor("dbg_scr0", [128, 16], F32, kind="ExternalOutput")
        dbg["x1f"] = nc.dram_tensor("dbg_x1f", [128, 8 * HALF_T], F32, kind="ExternalOutput")
        dbg["r20"] = nc.dram_tensor("dbg_r20", [128, D], F32, kind="ExternalOutput")
        dbg["oT0"] = nc.dram_tensor("dbg_oT0", [128, D], BF16, kind="ExternalOutput")

    xh_d = nc.dram_tensor("xh", [CORE_T + 6, D], F32, kind="ExternalInput")
    wq_d = nc.dram_tensor("wq", [D, D], F32, kind="ExternalInput")
    wk_d = nc.dram_tensor("wk", [D, D], F32, kind="ExternalInput")
    wv_d = nc.dram_tensor("wv", [D, D], F32, kind="ExternalInput")
    wo_d = nc.dram_tensor("wo", [D, D], F32, kind="ExternalInput")
    w1_d = nc.dram_tensor("w1", [NE, D, HF], F32, kind="ExternalInput")
    w2_d = nc.dram_tensor("w2", [NE, HF, D], F32, kind="ExternalInput")
    gw_d = nc.dram_tensor("gw_r", [128, NE * 8], F32, kind="ExternalInput")
    b1pc_d = nc.dram_tensor("b1_pc", [128, NE * 32], F32, kind="ExternalInput")
    bqkpc_d = nc.dram_tensor("bqk_pc", [128, 16], F32, kind="ExternalInput")
    b2_d = nc.dram_tensor("b2", [NE, D], F32, kind="ExternalInput")
    bnd_d = nc.dram_tensor("bias_nd", [65, 3 * D + 16], F32, kind="ExternalInput")
    masks_d = nc.dram_tensor("masks", [128, 3 * 134], F32, kind="ExternalInput")
    consts_d = nc.dram_tensor("consts", [128, 4], F32, kind="ExternalInput")
    ident_d = nc.dram_tensor("ident", [128, 128], F32, kind="ExternalInput")
    out_d = nc.dram_tensor("out", [CORE_T, D], F32, kind="ExternalOutput")

    with ExitStack() as S0:
        sb = lambda n, s, d: S0.enter_context(nc.sbuf_tensor("s_" + n, s, d))
        p = Prog(nc, S0)

        masks = sb("masks", [128, 3 * 134], F32)
        id_bf = sb("id_bf", [128, 128], BF16)
        id_f = sb("id_f", [128, 128], F32)
        bnd = sb("bnd", [65, 3 * D + 16], F32)
        consts = sb("consts", [128, 4], F32)
        b1pc = sb("b1pc", [128, NE * 32], F32)
        bqkpc = sb("bqkpc", [128, 16], F32)
        b2sb = sb("b2sb", [40, D], F32)
        gwb = sb("gwb", [128, NE * 8], BF16)
        xt1 = sb("xt1", [128, 8 * HALF_T], BF16)
        x1f = sb("x1f", [128, 8 * HALF_T], F32)
        comb = sb("comb", [128, TPH * NE], F32)
        St = sb("St", [128, 2 * NH], F32)
        rSt = sb("rSt", [128, 2 * NH], F32)
        scr = sb("scr", [128, 16], F32)
        smb = sb("smb", [128, 48], F32)
        r2 = sb("r2", [128, 2 * D], F32)
        ostg = sb("ostg", [128, 2 * D], F32)
        xb = sb("xb", [128, 2 * D], BF16)
        wbuf = [sb("wbuf0", [128, 8 * D], BF16), sb("wbuf1", [128, 8 * D], BF16)]

        eps_ap = consts[:, 0:1]
        neg_ap = consts[:, 1:2]

        def ones0(n):
            return bnd[0:1, D:D + n]

        def ones32(n):
            return bnd[32:33, D:D + n]

        def ones64(n):
            return bnd[64:65, D:D + n]

        ev_setup = None
        for dst, src in ((masks[:], masks_d.ap()), (bnd[:], bnd_d.ap()),
                         (consts[:], consts_d.ap()),
                         (b1pc[:], b1pc_d.ap()), (bqkpc[:], bqkpc_d.ap()),
                         (b2sb[32:40, 0:D], b2_d.ap()),
                         (id_f[:], ident_d.ap())):
            ev_setup = p.op("sp", lambda e, dst=dst, src=src: e.dma_start(dst, src),
                            inc="sSET", k=16)
        p.op("pool", lambda e: e.dma_start(id_bf[:], ident_d.ap()), inc="sSETp", k=16)
        p.op("pool", lambda e: e.dma_start(gwb[:], gw_d.ap()), inc="sSETp", k=16)
        ev_setup = ("sSET", p.v["sSET"])
        ev_setup_p = ("sSETp", p.v["sSETp"])
        ev_idbf = ev_setup_p
        ev_gwb = ev_setup_p
        p.flush("setup")

        store_ev = [None] * 16
        moe_last = {"mm1": None, "mm2": None, "stt10": None, "stt11": None,
                    "hrelu14": None, "hrelu15": None}
        pre = {}

        def emit_prefetch(h, weights=True):
            B_ = h * HALF_T
            for t in range(2):
                rows = 128
                src_ = xh_d.ap()[B_ + t * 128: B_ + t * 128 + rows, :]
                pre[(h, "xld", t)] = p.op(
                    "pool",
                    lambda e, t=t, src_=src_:
                    e.dma_start(xb[0:128, (t % 2) * D:(t % 2) * D + D], src_),
                    inc="sXB%d" % (t % 2), k=16)
            if not weights:
                return
            pre[(h, "wq")] = p.op("pool", lambda e: e.dma_start(
                wbuf[0][:].rearrange("p (c n) -> p c n", c=8),
                wq_d.ap().rearrange("(c p) n -> p c n", p=128)), inc="sWQ0", k=16)
            pre[(h, "wk")] = p.op("pool", lambda e: e.dma_start(
                wbuf[1][:].rearrange("p (c n) -> p c n", c=8),
                wk_d.ap().rearrange("(c p) n -> p c n", p=128)), inc="sWQ1", k=16)

        emit_prefetch(0, weights=False)

        for half in range(2):
            B = half * HALF_T

            with ExitStack() as S1:
                sb1 = lambda n, s, d: S1.enter_context(nc.sbuf_tensor("h%d_" % half + n, s, d))
                ps1 = lambda n, s, d: S1.enter_context(nc.psum_tensor("h%d_" % half + n, s, d))
                xt_ext = sb1("xt_ext", [128, 8 * EXT], BF16)
                qT = sb1("qT", [128, 8 * HALF_T], BF16)
                kT = sb1("kT", [128, 8 * EXT], BF16)
                v_tm = sb1("v_tm", [128, 9 * D], BF16)
                x_own = sb1("x_own", [128, 2 * D], F32)
                o_sb = sb1("o_sb", [128, 2 * D], BF16)
                oT = sb1("oT", [128, D], BF16)
                attnT = sb1("attnT", [128, 4 * 256], BF16)
                x1b = o_sb
                smx = sb1("smx", [128, 4 * 134], F32)
                a_bf = sb1("a_bf", [128, 4 * 134], BF16)
                nmx = sb1("nmx", [128, 4], F32)

                pt = [ps1("pt0", [128, 512], F32), ps1("pt1", [128, 512], F32)]
                ptb = [ps1("ptb0", [128, 256], BF16), ps1("ptb1", [128, 256], BF16)]
                pj = [ps1("pj0", [128, 512], F32), ps1("pj1", [128, 512], F32),
                      ps1("pj2", [128, 512], F32), ps1("pj3", [128, 512], F32)]

                xld = [None] * 9
                xtr_ev = {}
                xte_ev = {}

                def emit_xld(t):
                    rows = 128 if t < 8 else 6
                    src = xh_d.ap()[B + t * 128: B + t * 128 + rows, :]
                    waits = [xtr_ev[(t - 2, 7)]] if t >= 2 else []
                    xld[t] = p.op("pool",
                                  lambda e, t=t, rows=rows, src=src:
                                  e.dma_start(xb[0:rows, (t % 2) * D:(t % 2) * D + D], src),
                                  waits=waits, inc="sXB%d" % (t % 2), k=16)

                xld[0] = pre[(half, "xld", 0)]
                xld[1] = pre[(half, "xld", 1)]
                xown_ev = [None] * TPH
                for t in range(2):
                    src = xh_d.ap()[B + t * 128 + 3: B + t * 128 + 131, :]
                    xown_ev[t] = p.op("sp",
                                      lambda e, t=t, src=src:
                                      e.dma_start(x_own[:, (t % 2) * D:(t % 2) * D + D], src),
                                      inc="sXO%d" % (t % 2), k=16)

                for t in range(9):
                    rows = 128 if t < 8 else 6
                    for c in range(8):
                        waits = [xld[t], ev_idbf]
                        ev_e = xte_ev.get((t, c - 2)) if c >= 2 else xte_ev.get((t - 1, 6 + c))
                        if ev_e is not None:
                            waits.append(ev_e)
                        xtr_ev[(t, c)] = p.op(
                            "pe",
                            lambda e, t=t, c=c, rows=rows:
                            e.transpose(ptb[c % 2][0:128, 0:rows],
                                        xb[0:rows,
                                           (t % 2) * D + c * 128:(t % 2) * D + c * 128 + 128],
                                        id_bf[0:rows, 0:rows]),
                            waits=waits, inc="sPE")
                        xte_ev[(t, c)] = p.op(
                            "act",
                            lambda e, t=t, c=c, rows=rows:
                            e.copy(xt_ext[:, c * EXT + t * 128: c * EXT + t * 128 + rows],
                                   ptb[c % 2][0:128, 0:rows]),
                            waits=[xtr_ev[(t, c)]], inc="sACT")
                    if t + 2 < 9:
                        emit_xld(t + 2)
                if (half, "wq") in pre:
                    wq_ev = pre[(half, "wq")]
                    wk_ev = pre[(half, "wk")]
                else:
                    wq_ev = p.op("pool", lambda e: e.dma_start(
                        wbuf[0][:].rearrange("p (c n) -> p c n", c=8),
                        wq_d.ap().rearrange("(c p) n -> p c n", p=128)), inc="sWQ0", k=16)
                    wk_ev = p.op("pool", lambda e: e.dma_start(
                        wbuf[1][:].rearrange("p (c n) -> p c n", c=8),
                        wk_d.ap().rearrange("(c p) n -> p c n", p=128)), inc="sWQ1", k=16)
                p.flush("A1_h%d" % half)

                pj_ev = [None, None, None, None]
                pjn = [0]

                def proj_fm(widx, wait_w, oc, tok0, n, dst_ap_fn, bias_col):
                    bank = pjn[0] % 4
                    pjn[0] += 1
                    first_waits = [wait_w, pj_ev[bank], ev_setup]
                    last = None
                    for ic in range(8):
                        last = p.op(
                            "pe",
                            lambda e, ic=ic, bank=bank, widx=widx, oc=oc, tok0=tok0, n=n:
                            e.matmul(pj[bank][:, 0:n],
                                     wbuf[widx][:, ic * D + oc * 128: ic * D + oc * 128 + 128],
                                     xt_ext[:, ic * EXT + tok0: ic * EXT + tok0 + n],
                                     start=(ic == 0), stop=(ic == 7)),
                            waits=first_waits if ic == 0 else (),
                            inc="sPE" if ic == 7 else None)
                    ev = p.op("act",
                              lambda e, bank=bank, n=n, dst_ap_fn=dst_ap_fn, bias_col=bias_col:
                              e.activation(dst_ap_fn(), pj[bank][:, 0:n], AF.Identity,
                                           bias=bqkpc[:, bias_col:bias_col + 1], scale=1.0),
                              waits=[last], inc="sACT")
                    pj_ev[bank] = ev
                    return last, ev

                q_last_mm = None
                for oc in range(8):
                    for ts in range(2):
                        q_last_mm, _ = proj_fm(
                            0, wq_ev, oc, 3 + ts * 512, 512,
                            lambda oc=oc, ts=ts: qT[:, oc * HALF_T + ts * 512:
                                                    oc * HALF_T + ts * 512 + 512],
                            oc)
                k_last_mm = None
                for oc in range(8):
                    for ts, n in ((0, 512), (1, 512), (2, 6)):
                        k_last_mm, _ = proj_fm(
                            1, wk_ev, oc, ts * 512, n,
                            lambda oc=oc, ts=ts, n=n: kT[:, oc * EXT + ts * 512:
                                                         oc * EXT + ts * 512 + n],
                            8 + oc)

                wv_ev = p.op("pool", lambda e: e.dma_start(
                    wbuf[0][:].rearrange("p (c n) -> p c n", c=8),
                    wv_d.ap().rearrange("(c p) n -> p c n", p=128)),
                    waits=[q_last_mm], inc="sWQ0", k=16)
                wo_ev = p.op("pool", lambda e: e.dma_start(
                    wbuf[1][:].rearrange("p (c n) -> p c n", c=8),
                    wo_d.ap().rearrange("(c p) n -> p c n", p=128)),
                    waits=[k_last_mm], inc="sWQ1", k=16)

                for vt in range(5):
                    for fs in range(2):
                        bank = pjn[0] % 4
                        pjn[0] += 1
                        first_waits = [wv_ev, pj_ev[bank]]
                        last = None
                        for ic in range(8):
                            last = p.op(
                                "pe",
                                lambda e, ic=ic, bank=bank, vt=vt, fs=fs:
                                e.matmul(pj[bank][0:128, 0:512],
                                         xt_ext[:, ic * EXT + vt * 128: ic * EXT + vt * 128 + 128],
                                         wbuf[0][:, ic * D + fs * 512: ic * D + fs * 512 + 512],
                                         start=(ic == 0), stop=False),
                                waits=first_waits if ic == 0 else (), inc=None)
                        last = p.op(
                            "pe",
                            lambda e, bank=bank, fs=fs:
                            e.matmul(pj[bank][0:128, 0:512], ones64(128),
                                     bnd[64:65, fs * 512: fs * 512 + 512],
                                     start=False, stop=True),
                            waits=[ev_setup], inc="sPE")
                        ev = p.op("act",
                                  lambda e, bank=bank, vt=vt, fs=fs:
                                  e.copy(v_tm[0:128, vt * D + fs * 512: vt * D + fs * 512 + 512],
                                         pj[bank][0:128, 0:512]),
                                  waits=[last], inc="sACT")
                        pj_ev[bank] = ev
                vrdy = {}
                vchain_bank = [None, None]

                def vdrip(vt, guards):
                    rows = 128 if vt < 8 else 6
                    ev = None
                    for fs in range(2):
                        last = None
                        for ic in range(8):
                            last = p.op(
                                "pe",
                                lambda e, ic=ic, vt=vt, rows=rows, fs=fs:
                                e.matmul(pj[fs][0:rows, 0:512],
                                         xt_ext[:, ic * EXT + vt * 128: ic * EXT + vt * 128 + rows],
                                         wbuf[0][:, ic * D + fs * 512: ic * D + fs * 512 + 512],
                                         start=(ic == 0), stop=False),
                                waits=(guards + [wv_ev]) if ic == 0 else (), inc=None)
                        last = p.op(
                            "pe",
                            lambda e, rows=rows, fs=fs:
                            e.matmul(pj[fs][0:rows, 0:512], ones64(rows),
                                     bnd[64:65, fs * 512: fs * 512 + 512],
                                     start=False, stop=True),
                            waits=(), inc="sPE")
                        ev = p.op("act",
                                  lambda e, vt=vt, rows=rows, fs=fs:
                                  e.copy(v_tm[0:rows, vt * D + fs * 512: vt * D + fs * 512 + 512],
                                         pj[fs][0:rows, 0:512]),
                                  waits=[last], inc="sACT")
                        vchain_bank[fs] = ev
                    vrdy[vt] = ev
                p.flush("A2_h%d" % half)

                if debug and half == 0:
                    for key, src in (("qT", qT[:]), ("kT", kT[:]), ("v", v_tm[:]),
                                     ("xt0", xt_ext[:])):
                        p.op("sp", lambda e, key=key, src=src:
                             e.dma_start(dbg[key].ap(), src), inc="sO0", k=16)
                sc_ev = {}
                sm_ev = {}
                nmx_ev = {}
                exp_ev = {}
                rs_ev = {}
                t_ev = {}
                atc_ev = {}
                av_ev = {}
                oe_ev = {}
                ote_ev = {}
                oproj_ev = {}
                lnr_ev = {}
                x1_ev = [None] * TPH
                xte1_ev = {}
                xtr1_ev = {}
                otr_ev = {}
                x1b_evs = {}
                lgc_ev = [None] * TPH
                pm_ev = [None] * TPH
                msk2_ev = [None] * TPH
                comb_ev = [None] * TPH
                apply1_ev = [None] * TPH

                def sprev(t, h, d):
                    pb_ = (t // 2) * 2
                    s = 2 * h + (t % 2) - d
                    if s >= 0:
                        return (pb_ + s % 2, s // 2)
                    s += 32
                    return (pb_ - 2 + s % 2, s // 2)

                def emit_score(t, h):
                    ch, pb = h // 2, (h % 2) * 64
                    s_ = 2 * h + (t % 2)
                    sbank = (pj[0], pj[1], pt[0])[s_ % 3]
                    sl = (t % 2) * 2 + h % 2
                    hs = (t % 2) * 16 + h
                    moff = 0 if (half == 0 and t == 0) else (268 if (half == 1 and t == 7) else 134)
                    waits = [sm_ev.get(sprev(t, h, 3)),
                             vchain_bank[t % 2], ev_setup]
                    if s_ % 3 == 2:
                        waits.append(lgc_ev[(t // 2) * 2 - 1] if t >= 2 else None)
                    sc_ev[(t, h)] = p.op(
                        "pe",
                        lambda e, t=t, h=h, ch=ch, pb=pb, sbank=sbank:
                        e.matmul(sbank[:, 0:134],
                                 qT[pb:pb + 64, ch * HALF_T + t * 128: ch * HALF_T + t * 128 + 128],
                                 kT[pb:pb + 64, ch * EXT + t * 128: ch * EXT + t * 128 + 134],
                                 start=True, stop=True),
                        waits=waits, inc="sPE")
                    w2_ = [sc_ev[(t, h)],
                           exp_ev.get((t, h - 2)) if h >= 2 else exp_ev.get((t - 2, 14 + h))]
                    sm_ev[(t, h)] = p.op(
                        "dve",
                        lambda e, sbank=sbank, sl=sl, moff=moff:
                        e.scalar_tensor_tensor(smx[:, sl * 134: sl * 134 + 134],
                                               sbank[:, 0:134], consts[:, 2:3],
                                               masks[:, moff:moff + 134],
                                               AluOpType.mult, AluOpType.add),
                        waits=w2_, inc="sDVE")
                    nmx_ev[(t, h)] = p.op(
                        "dve",
                        lambda e, sl=sl:
                        e.reduce_max(nmx[:, sl:sl + 1],
                                     smx[:, sl * 134: sl * 134 + 134],
                                     axis=AX.X, negate=True),
                        waits=(), inc="sDVE")
                    w3_ = [sm_ev[(t, h)], nmx_ev[(t, h)],
                           t_ev.get((t, h - 2)) if h >= 2 else t_ev.get((t - 2, 14 + h)),
                           rs_ev.get((t - 2, 15))]
                    exp_ev[(t, h)] = p.op(
                        "act",
                        lambda e, sl=sl, hs=hs:
                        e.activation(a_bf[:, sl * 134: sl * 134 + 134],
                                     smx[:, sl * 134: sl * 134 + 134],
                                     AF.Exp, bias=nmx[:, sl:sl + 1], scale=1.0,
                                     accum_out=St[:, hs:hs + 1]),
                        waits=w3_, inc="sACT")

                def emit_tav(t, h):
                    bk = t % 2
                    sl = (t % 2) * 2 + h % 2
                    hs = (t % 2) * 16 + h
                    at = (t % 2) * 512 + (h % 2) * 256
                    prev1 = (t // 2) * 2 - 1
                    rs_ev[(t, h)] = p.op(
                        "dve",
                        lambda e, hs=hs: e.reciprocal(rSt[:, hs:hs + 1], St[:, hs:hs + 1]),
                        waits=[exp_ev[(t, h)], oe_ev.get((t - 2, 15))], inc="sDVE")
                    wt = [exp_ev[(t, h)],
                          atc_ev.get((t, h - 1)) if h >= 1 else xte1_ev.get((prev1, 6 + t % 2))]
                    p.op("pe",
                         lambda e, bk=bk, sl=sl:
                         e.transpose(ptb[bk][:, 0:128],
                                     a_bf[:, sl * 134: sl * 134 + 128], id_bf[:]),
                         waits=wt, inc=None)
                    t_ev[(t, h)] = p.op(
                        "pe",
                        lambda e, bk=bk, sl=sl:
                        e.transpose(ptb[bk][0:6, 128:256],
                                    a_bf[:, sl * 134 + 128: sl * 134 + 134], id_bf[:]),
                        waits=(), inc="sPE")
                    wc = [t_ev[(t, h)],
                          av_ev.get((t, h - 2)) if h >= 2 else av_ev.get((t - 2, 14 + h))]
                    p.op("dve",
                         lambda e, bk=bk, at=at:
                         e.tensor_copy(attnT[:, at: at + 128], ptb[bk][:, 0:128]),
                         waits=wc, inc=None)
                    atc_ev[(t, h)] = p.op(
                        "dve",
                        lambda e, bk=bk, at=at:
                        e.tensor_copy(attnT[0:6, at + 128: at + 256], ptb[bk][0:6, 128:256]),
                        waits=(), inc="sDVE")
                    wav = [atc_ev[(t, h)],
                           oe_ev.get((t, h - 1)) if h >= 1 else oe_ev.get((t - 2, 15)),
                           vrdy.get(t), vrdy.get(t + 1)]
                    p.op("pe",
                         lambda e, t=t, h=h, bk=bk, at=at:
                         e.matmul(pj[2 + bk][:, 0:64],
                                  attnT[:, at: at + 128],
                                  v_tm[:, t * D + h * 64: t * D + h * 64 + 64],
                                  start=True, stop=False),
                         waits=wav, inc=None)
                    av_ev[(t, h)] = p.op(
                        "pe",
                        lambda e, t=t, h=h, bk=bk, at=at:
                        e.matmul(pj[2 + bk][:, 0:64],
                                 attnT[0:6, at + 128: at + 256],
                                 v_tm[0:6, (t + 1) * D + h * 64:(t + 1) * D + h * 64 + 64],
                                 start=False, stop=True),
                        waits=(), inc="sPE")
                    oe_ev[(t, h)] = p.op(
                        "dve",
                        lambda e, t=t, h=h, bk=bk, hs=hs:
                        e.tensor_single_scalar(
                            o_sb[:, (t % 2) * D + h * 64:(t % 2) * D + h * 64 + 64],
                            pj[2 + bk][:, 0:64], rSt[:, hs:hs + 1], AluOpType.mult),
                        waits=[av_ev[(t, h)], rs_ev[(t, h)]], inc="sDVE")

                def emit_oT(t):
                    pb0 = (t // 2) * 2
                    for c in range(8):
                        wt = [oe_ev[(t, NH - 1)], atc_ev[(pb0 + c % 2, NH - 1)]]
                        if c >= 2:
                            wt.append(ote_ev[(t, c - 2)])
                        elif t >= 1:
                            wt.append(xte1_ev.get((t - 1, 6 + c)))
                        if t >= 1:
                            wt.append(oproj_ev.get((t - 1, 1)))
                        otr_ev[(t, c)] = ev = p.op("pe",
                                  lambda e, t=t, c=c:
                                  e.transpose(ptb[c % 2][:, 0:128],
                                              o_sb[:, (t % 2) * D + c * 128:(t % 2) * D + c * 128 + 128],
                                              id_bf[:]),
                                  waits=wt, inc="sPE")
                        ote_ev[(t, c)] = p.op(
                            "dve",
                            lambda e, t=t, c=c:
                            e.tensor_copy(oT[:, c * 128: c * 128 + 128],
                                          ptb[c % 2][:, 0:128]),
                            waits=[ev, oproj_ev.get((t - 1, 1))], inc="sDVE")

                def emit_pair_heads(t0, t1):
                    emit_score(t0, 0)
                    emit_score(t1, 0)
                    for h in range(NH):
                        if h + 1 < NH:
                            emit_score(t0, h + 1)
                            emit_score(t1, h + 1)
                        emit_tav(t0, h)
                        emit_tav(t1, h)

                for pr in range(TPH // 2):
                  t0_, t1_ = 2 * pr, 2 * pr + 1
                  emit_pair_heads(t0_, t1_)
                  if debug and half == 0 and pr == 0:
                      p.op("sp", lambda e: e.dma_start(dbg["o0"].ap(), o_sb[:, 0:D]),
                           waits=[oe_ev[(0, NH - 1)]], inc="sO0", k=16)
                  for t in (t0_, t1_):
                    emit_oT(t)

                    for fs in range(2):
                        wts = [ote_ev[(t, 7)], wo_ev]
                        if t >= 1:
                            wts += [lgc_ev[t - 1], lnr_ev[(t - 1, fs)]]
                        if fs == 0:
                            wts.append(sm_ev[((t // 2) * 2 + 1, 15)])
                        last = None
                        for ic in range(8):
                            last = p.op(
                                "pe",
                                lambda e, ic=ic, fs=fs:
                                e.matmul(pt[fs][:, 0:512],
                                         oT[:, ic * 128: ic * 128 + 128],
                                         wbuf[1][:, ic * D + fs * 512: ic * D + fs * 512 + 512],
                                         start=(ic == 0), stop=False),
                                waits=wts if ic == 0 else (), inc=None)
                        oproj_ev[(t, fs)] = p.op(
                            "pe",
                            lambda e, fs=fs:
                            e.matmul(pt[fs][:, 0:512], ones0(128),
                                     bnd[0:1, 2 * D + fs * 512: 2 * D + fs * 512 + 512],
                                     start=False, stop=True),
                            waits=(), inc="sPE")

                    if 2 <= t <= 5:
                        vdrip(t + 3, [sm_ev[(t0_, 15)], sm_ev[(t1_, 15)]])

                    for fs in range(2):
                        wl = [oproj_ev[(t, fs)], xown_ev[t]]
                        if t >= 2:
                            wl.append(apply1_ev[t - 2])
                            wl.append(x1b_evs.get(t - 2))
                        lnr_ev[(t, fs)] = p.op(
                            "dve",
                            lambda e, t=t, fs=fs, sp8=(t % 2) * 8:
                            e.scalar_tensor_tensor(
                                r2[:, (t % 2) * D + fs * 512:(t % 2) * D + fs * 512 + 512],
                                pt[fs][:, 0:512], 0.0,
                                x_own[:, (t % 2) * D + fs * 512:(t % 2) * D + fs * 512 + 512],
                                AluOpType.add, AluOpType.add,
                                accum_out=scr[:, sp8 + fs:sp8 + fs + 1]),
                            waits=wl, inc="sDVE")
                    if t + 2 < TPH:
                        src = xh_d.ap()[B + (t + 2) * 128 + 3: B + (t + 2) * 128 + 131, :]
                        xown_ev[t + 2] = p.op(
                            "sp",
                            lambda e, t=t, src=src:
                            e.dma_start(x_own[:, (t % 2) * D:(t % 2) * D + D], src),
                            waits=[lnr_ev[(t, 0)], lnr_ev[(t, 1)]],
                            inc="sXO%d" % (t % 2), k=16)
                    sp8 = (t % 2) * 8
                    s1_ev = p.op("dve",
                                 lambda e, sp8=sp8: e.tensor_add(scr[:, sp8 + 2:sp8 + 3],
                                                                 scr[:, sp8 + 0:sp8 + 1],
                                                                 scr[:, sp8 + 1:sp8 + 2]),
                                 waits=[lnr_ev[(t, 0)], lnr_ev[(t, 1)]], inc="sDVE")
                    nm_ev = p.op("dve",
                                 lambda e, sp8=sp8:
                                 e.tensor_single_scalar(scr[:, sp8 + 3:sp8 + 4],
                                                        scr[:, sp8 + 2:sp8 + 3],
                                                        consts[:, 3:4], AluOpType.mult),
                                 waits=[s1_ev], inc="sDVE")
                    cen_ev = p.op(
                        "dve",
                        lambda e, t=t, sp8=sp8:
                        e.tensor_single_scalar(r2[:, (t % 2) * D:(t % 2) * D + D],
                                               r2[:, (t % 2) * D:(t % 2) * D + D],
                                               scr[:, sp8 + 3:sp8 + 4], AluOpType.add),
                        waits=[nm_ev], inc="sDVE")
                    sq_w = [cen_ev]
                    if store_ev[(half * 8 + t) - 2] is not None and stage == "full":
                        sq_w.append(store_ev[(half * 8 + t) - 2])
                    sq_ev = p.op(
                        "act",
                        lambda e, t=t, sp8=sp8:
                        e.activation(ostg[:, (t % 2) * D:(t % 2) * D + D],
                                     r2[:, (t % 2) * D:(t % 2) * D + D],
                                     AF.Square, accum_out=scr[:, sp8 + 4:sp8 + 5]),
                        waits=sq_w, inc="sACT")
                    ln_ev = p.op("act",
                         lambda e, sp8=sp8: e.activation(scr[:, sp8 + 5:sp8 + 6],
                                                         scr[:, sp8 + 4:sp8 + 5], AF.Ln,
                                                         scale=1.0 / D, bias=eps_ap),
                         waits=[sq_ev], inc="sACT")
                    rstd_ev = p.op("act",
                         lambda e, sp8=sp8: e.activation(scr[:, sp8 + 6:sp8 + 7],
                                                         scr[:, sp8 + 5:sp8 + 6],
                                                         AF.Exp, scale=-0.5),
                         waits=[ln_ev], inc="sACT")
                    apply1_ev[t] = p.op(
                        "act",
                        lambda e, t=t, sp8=sp8:
                        e.mul(x1f[:, t * D: t * D + D], r2[:, (t % 2) * D:(t % 2) * D + D],
                              scr[:, sp8 + 6:sp8 + 7]),
                        waits=[rstd_ev], inc="sACT")

                    if debug and half == 0 and t == 0:
                        p.op("sp", lambda e: e.dma_start(dbg["scr0"].ap(), scr[:]),
                             waits=[apply1_ev[0]], inc="sO0", k=16)
                        p.op("sp", lambda e: e.dma_start(dbg["r20"].ap(), r2[:, 0:D]),
                             waits=[apply1_ev[0]], inc="sO0", k=16)
                        p.op("sp", lambda e: e.dma_start(dbg["oT0"].ap(), oT[:]),
                             waits=[oproj_ev[(0, 1)]], inc="sO0", k=16)

                    wcp = [apply1_ev[t], otr_ev[(t, 7)], xtr1_ev.get((t - 2, 7))]
                    x1b_a = p.op("pool",
                                 lambda e, t=t:
                                 e.tensor_copy(x1b[:, (t % 2) * D:(t % 2) * D + 512],
                                               x1f[:, t * D: t * D + 512]),
                                 waits=wcp, inc="sPL")
                    x1b_ev = p.op("pool",
                                  lambda e, t=t:
                                  e.tensor_copy(x1b[:, (t % 2) * D + 512:(t % 2) * D + D],
                                                x1f[:, t * D + 512: t * D + D]),
                                  waits=(), inc="sPL")
                    x1b_evs[t] = x1b_ev
                    for c in range(8):
                        wt = [x1b_a if c < 4 else x1b_ev]
                        if c >= 2:
                            wt.append(xte1_ev[(t, c - 2)])
                        else:
                            wt.append(ote_ev[(t, 6 + c)])
                        xtr1_ev[(t, c)] = p.op(
                            "pe",
                            lambda e, t=t, c=c:
                            e.transpose(ptb[c % 2][:, 0:128],
                                        x1b[:, (t % 2) * D + c * 128:(t % 2) * D + c * 128 + 128],
                                        id_bf[:]),
                            waits=wt, inc="sPE")
                        xte1_ev[(t, c)] = p.op(
                            "act",
                            lambda e, t=t, c=c:
                            e.copy(xt1[:, c * HALF_T + t * 128: c * HALF_T + t * 128 + 128],
                                   ptb[c % 2][:, 0:128]),
                            waits=[xtr1_ev[(t, c)]], inc="sACT")

                    glast = None
                    for ic in range(8):
                        glast = p.op(
                            "pe",
                            lambda e, t=t, ic=ic:
                            e.matmul(pt[0][:, 0:8],
                                     xt1[:, ic * HALF_T + t * 128: ic * HALF_T + t * 128 + 128],
                                     gwb[:, ic * 8: ic * 8 + 8],
                                     start=(ic == 0), stop=False),
                            waits=[xte1_ev[(t, 7)], ev_gwb] if ic == 0 else (), inc=None)
                    glast = p.op("pe",
                                 lambda e: e.matmul(pt[0][:, 0:8], ones32(128),
                                                    bnd[32:33, 3 * D: 3 * D + 8],
                                                    start=False, stop=True),
                                 waits=(), inc="sPE")
                    wlg = [glast]
                    if t >= 1:
                        wlg += [msk2_ev[t - 1], pm_ev[t - 1]]
                    lgc_ev[t] = p.op("act",
                                     lambda e: e.copy(smb[:, 0:8], pt[0][:, 0:8]),
                                     waits=wlg, inc="sACT")
                    m1_ev = p.op("dve",
                                 lambda e: e.reduce_max(smb[:, 40:41], smb[:, 0:8], axis=AX.X),
                                 waits=[lgc_ev[t]], inc="sDVE")
                    nm1_ev = p.op("dve",
                                  lambda e: e.reduce_max(smb[:, 41:42], smb[:, 0:8],
                                                         axis=AX.X, negate=True),
                                  waits=(), inc="sDVE")
                    msk_ev = p.op("dve",
                                  lambda e: e.tensor_single_scalar(smb[:, 8:16], smb[:, 0:8],
                                                                   smb[:, 40:41], AluOpType.is_ge),
                                  waits=[m1_ev], inc="sDVE")
                    l2_ev = p.op("dve",
                                 lambda e: e.scalar_tensor_tensor(smb[:, 8:16], smb[:, 8:16],
                                                                  neg_ap, smb[:, 0:8],
                                                                  AluOpType.mult, AluOpType.add),
                                 waits=[ev_setup, msk_ev], inc="sDVE")
                    m2_ev = p.op("dve",
                                 lambda e: e.reduce_max(smb[:, 42:43], smb[:, 8:16], axis=AX.X),
                                 waits=[l2_ev], inc="sDVE")
                    wpe = [nm1_ev, lgc_ev[t]]
                    if t >= 1:
                        wpe.append(pm_ev[t - 1])
                    pexp_ev = p.op("act",
                                   lambda e: e.activation(smb[:, 16:24], smb[:, 0:8], AF.Exp,
                                                          bias=smb[:, 41:42], scale=1.0),
                                   waits=wpe, inc="sACT")
                    msk2_ev[t] = p.op("dve",
                                      lambda e: e.tensor_single_scalar(smb[:, 24:32], smb[:, 0:8],
                                                                       smb[:, 42:43],
                                                                       AluOpType.is_ge),
                                      waits=[m2_ev], inc="sDVE")
                    pm_ev[t] = p.op("dve",
                                    lambda e: e.tensor_mul(smb[:, 32:40], smb[:, 16:24],
                                                           smb[:, 24:32]),
                                    waits=[pexp_ev, msk2_ev[t]], inc="sDVE")
                    z_ev = p.op("dve",
                                lambda e: e.reduce_sum(smb[:, 43:44], smb[:, 32:40], axis=AX.X),
                                waits=[pm_ev[t]], inc="sDVE")
                    rz_ev = p.op("dve",
                                 lambda e: e.reciprocal(smb[:, 44:45], smb[:, 43:44]),
                                 waits=[z_ev], inc="sDVE")
                    comb_ev[t] = p.op(
                        "dve",
                        lambda e, t=t:
                        e.tensor_single_scalar(comb[:, t * 8: t * 8 + 8], smb[:, 32:40],
                                               smb[:, 44:45], AluOpType.mult),
                        waits=[rz_ev], inc="sDVE")
                p.flush("A3_h%d" % half)

            if stage == "ln1":
                if debug and half == 0:
                    p.op("sp", lambda e: e.dma_start(dbg["x1f"].ap(), x1f[:]),
                         inc="sO0", k=16)
                for t in range(TPH):
                    g = half * 8 + t
                    store_ev[g] = p.op(
                        "sp",
                        lambda e, t=t:
                        e.dma_start(out_d.ap()[half * HALF_T + t * 128:
                                               half * HALF_T + t * 128 + 128, :],
                                    x1f[:, t * D: t * D + D]),
                        waits=[apply1_ev[t]], inc="sO%d" % (t % 2), k=16)
                p.flush("ST_h%d" % half)
                continue

            with ExitStack() as S2:
                sb2 = lambda n, s, d: S2.enter_context(nc.sbuf_tensor("m%d_" % half + n, s, d))
                ps2 = lambda n, s, d: S2.enter_context(nc.psum_tensor("m%d_" % half + n, s, d))
                w1b = [sb2("w1b0", [128, NJ * GCOLS], BF16)]
                w2b = [sb2("w2b0", [128, NJ * GCOLS], BF16)]
                hg = sb2("hg", [128, NJ * HALF_T], BF16)
                cbT = sb2("cbT", [40, D], F32)
                acc = sb2("acc", [128, TPH * D], F32)
                pH = [ps2("pH0", [128, 512], F32), ps2("pH1", [128, 512], F32)]
                pY = [[ps2("pY00", [128, 512], F32), ps2("pY01", [128, 512], F32)],
                      [ps2("pY10", [128, 512], F32), ps2("pY11", [128, 512], F32)]]

                ctb_ev = {}
                accin_ev = {}
                b2mm_ev = {}
                for t in range(TPH):
                    wt = [comb_ev[t]]
                    if t >= 2:
                        wt.append(ctb_ev[t - 2])
                    ev = p.op("pe",
                              lambda e, t=t:
                              e.transpose(pH[t % 2][0:8, 0:128], comb[:, t * 8: t * 8 + 8],
                                          id_f[:]),
                              waits=wt, inc="sPE")
                    ctb_ev[t] = p.op(
                        "act",
                        lambda e, t=t:
                        e.copy(cbT[32:40, t * 128: t * 128 + 128], pH[t % 2][0:8, 0:128]),
                        waits=[ev], inc="sACT")
                    for fs in range(2):
                        wb = [ctb_ev[t]]
                        if t >= 2:
                            wb.append(accin_ev[(t - 2, fs)])
                        b2mm_ev[(t, fs)] = p.op(
                            "pe",
                            lambda e, t=t, fs=fs:
                            e.matmul(pY[t % 2][fs][:, 0:512],
                                     cbT[32:40, t * 128: t * 128 + 128],
                                     b2sb[32:40, fs * 512: fs * 512 + 512],
                                     start=True, stop=True),
                            waits=wb, inc="sPE")
                        accin_ev[(t, fs)] = p.op(
                            "act",
                            lambda e, t=t, fs=fs:
                            e.copy(acc[:, t * D + fs * 512: t * D + fs * 512 + 512],
                                   pY[t % 2][fs][:, 0:512]),
                            waits=[b2mm_ev[(t, fs)]], inc="sACT")

                apply2_ev = [None] * TPH

                def emit_l2(t):
                    g = half * 8 + t
                    wr = [stt_ev[(31, t, 0)], stt_ev[(31, t, 1)]]
                    if t >= 2:
                        wr.append(apply2_ev[t - 2])
                    sp8 = (t % 2) * 8
                    r2s_ev = p.op(
                        "dve",
                        lambda e, t=t, sp8=sp8:
                        e.scalar_tensor_tensor(r2[:, (t % 2) * D:(t % 2) * D + D],
                                               x1f[:, t * D: t * D + D], 0.0,
                                               acc[:, t * D: t * D + D],
                                               AluOpType.add, AluOpType.add,
                                               accum_out=scr[:, sp8 + 2:sp8 + 3]),
                        waits=wr, inc="sDVE")
                    nm_ev = p.op("dve",
                                 lambda e, sp8=sp8:
                                 e.tensor_single_scalar(scr[:, sp8 + 3:sp8 + 4],
                                                        scr[:, sp8 + 2:sp8 + 3],
                                                        consts[:, 3:4], AluOpType.mult),
                                 waits=[r2s_ev], inc="sDVE")
                    cen_ev = p.op(
                        "dve",
                        lambda e, t=t, sp8=sp8:
                        e.tensor_single_scalar(r2[:, (t % 2) * D:(t % 2) * D + D],
                                               r2[:, (t % 2) * D:(t % 2) * D + D],
                                               scr[:, sp8 + 3:sp8 + 4], AluOpType.add),
                        waits=[nm_ev], inc="sDVE")
                    sq_w = [cen_ev]
                    if g >= 2 and store_ev[g - 2] is not None:
                        sq_w.append(store_ev[g - 2])
                    sq2_ev = p.op("act",
                         lambda e, t=t, sp8=sp8:
                         e.activation(ostg[:, (t % 2) * D:(t % 2) * D + D],
                                      r2[:, (t % 2) * D:(t % 2) * D + D],
                                      AF.Square, accum_out=scr[:, sp8 + 4:sp8 + 5]),
                         waits=sq_w, inc="sACT")
                    ln2_ev = p.op("act",
                         lambda e, sp8=sp8: e.activation(scr[:, sp8 + 5:sp8 + 6],
                                                         scr[:, sp8 + 4:sp8 + 5], AF.Ln,
                                                         scale=1.0 / D, bias=eps_ap),
                         waits=[sq2_ev], inc="sACT")
                    rstd2_ev = p.op("act",
                         lambda e, sp8=sp8: e.activation(scr[:, sp8 + 6:sp8 + 7],
                                                         scr[:, sp8 + 5:sp8 + 6],
                                                         AF.Exp, scale=-0.5),
                         waits=[ln2_ev], inc="sACT")
                    apply2_ev[t] = p.op(
                        "act",
                        lambda e, t=t, sp8=sp8:
                        e.mul(ostg[:, (t % 2) * D:(t % 2) * D + D],
                              r2[:, (t % 2) * D:(t % 2) * D + D], scr[:, sp8 + 6:sp8 + 7]),
                        waits=[rstd2_ev], inc="sACT")
                    store_ev[g] = p.op(
                        "sp",
                        lambda e, t=t, half=half:
                        e.dma_start(out_d.ap()[half * HALF_T + t * 128:
                                               half * HALF_T + t * 128 + 128, :],
                                    ostg[:, (t % 2) * D:(t % 2) * D + D]),
                        waits=[apply2_ev[t]], inc="sO%d" % (t % 2), k=16)

                h_ev = {}
                hrelu_ev = {}
                y_ev = {}
                stt_ev = {}
                w1_ev = {}
                w2_ev = {}
                for e_i in range(NE):
                    for g_i in range(NG):
                        step = e_i * NG + g_i
                        par = step % 2
                        wl = []
                        if step >= 1:
                            wl.append(h_ev[(step - 1, 15)])
                        elif moe_last["mm1"] is not None:
                            wl.append(moe_last["mm1"])
                        w1_ev[step] = p.op(
                            "pool",
                            lambda e, e_i=e_i, g_i=g_i:
                            e.dma_start(
                                w1b[0][:].rearrange("p (c n) -> p c n", c=8),
                                w1_d.ap()[e_i, :, g_i * GCOLS:(g_i + 1) * GCOLS]
                                .rearrange("(c p) n -> p c n", p=128)),
                            waits=wl, inc="sW1%d" % (step % 2), k=16)
                        wl = []
                        if step >= 1:
                            wl.append(y_ev[(step - 1, 7, 1)])
                        elif moe_last["mm2"] is not None:
                            wl.append(moe_last["mm2"])
                        w2_ev[step] = p.op(
                            "pool",
                            lambda e, e_i=e_i, g_i=g_i:
                            e.dma_start(
                                w2b[0][:].rearrange("p (c n) -> p c n", c=8),
                                w2_d.ap()[e_i, g_i * GCOLS:(g_i + 1) * GCOLS, :]
                                .rearrange("(c p) n -> p c n", p=128)),
                            waits=wl, inc="sW2%d" % (step % 2), k=16)

                        for j in range(NJ):
                            for ts in range(2):
                                jj = j * 2 + ts
                                wm = [w1_ev[step]]
                                if jj >= 2:
                                    wm.append(hrelu_ev[(step, jj - 2)])
                                elif step >= 1:
                                    wm.append(hrelu_ev[(step - 1, 14 + jj)])
                                else:
                                    wm.append(ctb_ev[6 + jj])
                                    wm.append(moe_last["hrelu14" if jj == 0 else "hrelu15"])
                                last = None
                                for ic in range(8):
                                    last = p.op(
                                        "pe",
                                        lambda e, ic=ic, j=j, ts=ts, par=par, jj=jj:
                                        e.matmul(pH[jj % 2][:, 0:512],
                                                 w1b[0][:, ic * GCOLS + j * 128:
                                                        ic * GCOLS + j * 128 + 128],
                                                 xt1[:, ic * HALF_T + ts * 512:
                                                     ic * HALF_T + ts * 512 + 512],
                                                 start=(ic == 0), stop=(ic == 7)),
                                        waits=wm if ic == 0 else (),
                                        inc="sPE" if ic == 7 else None)
                                h_ev[(step, jj)] = last
                                wh = [last]
                                if step >= 1:
                                    wh.append(y_ev[(step - 1, 7, 1)])
                                elif moe_last["mm2"] is not None:
                                    wh.append(moe_last["mm2"])
                                hrelu_ev[(step, jj)] = p.op(
                                    "act",
                                    lambda e, j=j, ts=ts, jj=jj, e_i=e_i, g_i=g_i:
                                    e.activation(hg[:, j * HALF_T + ts * 512:
                                                    j * HALF_T + ts * 512 + 512],
                                                 pH[jj % 2][:, 0:512], AF.Relu,
                                                 bias=b1pc[:, e_i * 32 + g_i * 8 + j:
                                                           e_i * 32 + g_i * 8 + j + 1],
                                                 scale=1.0),
                                    waits=wh, inc="sACT")

                        for t in range(TPH):
                            wm = [w2_ev[step], hrelu_ev[(step, 15)]]
                            if t >= 2:
                                wm += [stt_ev[(step, t - 2, 0)], stt_ev[(step, t - 2, 1)]]
                            elif step >= 1:
                                wm += [stt_ev[(step - 1, 6 + t, 0)], stt_ev[(step - 1, 6 + t, 1)]]
                            else:
                                wm += [moe_last["stt1" + str(t)] or ev_setup]
                            if step == 0:
                                wm += [accin_ev[(6 + t % 2, 0)], accin_ev[(6 + t % 2, 1)]]
                            for j in range(NJ):
                                for fs in range(2):
                                    last = p.op(
                                        "pe",
                                        lambda e, t=t, j=j, fs=fs, par=par:
                                        e.matmul(pY[t % 2][fs][:, 0:512],
                                                 hg[:, j * HALF_T + t * 128:
                                                    j * HALF_T + t * 128 + 128],
                                                 w2b[0][:, j * GCOLS + fs * 512:
                                                        j * GCOLS + fs * 512 + 512],
                                                 start=(j == 0), stop=(j == NJ - 1)),
                                        waits=wm if (j == 0 and fs == 0) else (),
                                        inc="sPE" if j == NJ - 1 else None)
                                    if j == NJ - 1:
                                        y_ev[(step, t, fs)] = last
                            for fs in range(2):
                                ws = [y_ev[(step, t, fs)], accin_ev[(t, fs)], comb_ev[t]]
                                stt_ev[(step, t, fs)] = p.op(
                                    "dve",
                                    lambda e, t=t, fs=fs, e_i=e_i:
                                    e.scalar_tensor_tensor(
                                        acc[:, t * D + fs * 512: t * D + fs * 512 + 512],
                                        pY[t % 2][fs][:, 0:512],
                                        comb[:, t * 8 + e_i: t * 8 + e_i + 1],
                                        acc[:, t * D + fs * 512: t * D + fs * 512 + 512],
                                        AluOpType.mult, AluOpType.add),
                                    waits=ws, inc="sDVE")
                            if step == 31:
                                emit_l2(t)
                if half == 0:
                    emit_prefetch(1)
                moe_last["mm1"] = h_ev[(31, 15)]
                moe_last["mm2"] = y_ev[(31, 7, 1)]
                moe_last["hrelu14"] = hrelu_ev[(31, 14)]
                moe_last["hrelu15"] = hrelu_ev[(31, 15)]
                moe_last["stt10"] = stt_ev[(31, 6, 1)]
                moe_last["stt11"] = stt_ev[(31, 7, 1)]
                p.flush("ML2_h%d" % half)

        p.op("sp", lambda e: e.nop(),
             waits=[("sO0", p.v["sO0"]), ("sO1", p.v["sO1"])])
        p.flush("fin")
    return nc


def _make_masks(chunk):
    i = np.arange(128)[:, None]
    w = np.arange(134)[None, :]
    band = np.where((w - i >= 0) & (w - i <= 6), 0.0, NEG).astype(np.float32)
    mF = np.where(w >= 3, band, NEG).astype(np.float32) if chunk == 0 else band
    mL = np.where(w <= 130, band, NEG).astype(np.float32) if chunk == 3 else band
    return np.concatenate([mF, band, mL], axis=1)


def kernel(L0_attn_hidden_states, L0_attn_w_q, L0_attn_b_q, L0_attn_w_k, L0_attn_b_k,
           L0_attn_w_v, L0_attn_b_v, L0_attn_w_o, L0_attn_b_o, L0_ffn_gate_w,
           L0_ffn_gate_b, L0_ffn_w1, L0_ffn_b1, L0_ffn_w2, L0_ffn_b2,
           _stage="full"):
    f32 = np.float32
    x = np.ascontiguousarray(np.asarray(L0_attn_hidden_states, dtype=f32)).reshape(16384, D)

    bnd = np.zeros((65, 3 * D + 16), dtype=f32)
    bnd[0, 0:D] = np.asarray(L0_attn_b_q, dtype=f32)
    bnd[32, 0:D] = np.asarray(L0_attn_b_k, dtype=f32)
    bnd[64, 0:D] = np.asarray(L0_attn_b_v, dtype=f32)
    bnd[0, D:2 * D] = 1.0
    bnd[32, D:2 * D] = 1.0
    bnd[64, D:2 * D] = 1.0
    bnd[0, 2 * D:3 * D] = np.asarray(L0_attn_b_o, dtype=f32)
    bnd[32, 3 * D:3 * D + 8] = np.asarray(L0_ffn_gate_b, dtype=f32)
    gw_r = np.ascontiguousarray(
        np.asarray(L0_ffn_gate_w, dtype=f32).reshape(8, 128, 8).transpose(1, 0, 2).reshape(128, 64))
    b1_pc = np.ascontiguousarray(
        np.asarray(L0_ffn_b1, dtype=f32).reshape(8, 32, 128).transpose(2, 0, 1).reshape(128, 256))
    consts = np.zeros((128, 4), dtype=f32)
    consts[:, 0] = 1e-5
    consts[:, 1] = NEG
    consts[:, 2] = 0.125
    consts[:, 3] = -1.0 / 1024.0
    bqk_pc = np.zeros((128, 16), dtype=f32)
    bqk_pc[:, 0:8] = np.asarray(L0_attn_b_q, dtype=f32).reshape(8, 128).T
    bqk_pc[:, 8:16] = np.asarray(L0_attn_b_k, dtype=f32).reshape(8, 128).T
    ident = np.eye(128, dtype=f32)

    shared = {
        "wq": np.ascontiguousarray(np.asarray(L0_attn_w_q, dtype=f32)),
        "wk": np.ascontiguousarray(np.asarray(L0_attn_w_k, dtype=f32)),
        "wv": np.ascontiguousarray(np.asarray(L0_attn_w_v, dtype=f32)),
        "wo": np.ascontiguousarray(np.asarray(L0_attn_w_o, dtype=f32)),
        "w1": np.ascontiguousarray(np.asarray(L0_ffn_w1, dtype=f32)),
        "w2": np.ascontiguousarray(np.asarray(L0_ffn_w2, dtype=f32)),
        "gw_r": gw_r, "b1_pc": b1_pc, "bqk_pc": bqk_pc,
        "b2": np.ascontiguousarray(np.asarray(L0_ffn_b2, dtype=f32)),
        "bias_nd": bnd, "consts": consts, "ident": ident,
    }

    in_maps = []
    for c in range(8):
        batch, chunk = c // 4, c % 4
        r0 = batch * SEQ + chunk * CORE_T
        xh = np.zeros((CORE_T + 6, D), dtype=f32)
        lo = max(r0 - 3, batch * SEQ)
        hi = min(r0 + CORE_T + 3, (batch + 1) * SEQ)
        xh[lo - (r0 - 3): hi - (r0 - 3), :] = x[lo:hi, :]
        m = dict(shared)
        m["xh"] = xh
        m["masks"] = _make_masks(chunk)
        in_maps.append(m)

    nc = build_nc(stage=_stage)
    res = run_bass_kernel_spmd(nc, in_maps, core_ids=list(range(8)))
    out = np.empty((16384, D), dtype=f32)
    for c in range(8):
        out[c * CORE_T:(c + 1) * CORE_T, :] = res.results[c]["out"]
    return out.reshape(2, SEQ, D)
